# version 25
# baseline (speedup 1.0000x reference)
import numpy as np
import ml_dtypes
from contextlib import ExitStack

import concourse.bass as bass
import concourse.mybir as mybir
from concourse.bass_utils import run_bass_kernel_spmd
from concourse.alu_op_type import AluOpType as OP

F32 = mybir.dt.float32
BF16 = mybir.dt.bfloat16
AF = mybir.ActivationFunctionType
AX = mybir.AxisListType

S = 2048
D = 1024
FF = 4096
NT = 16
NC = 4
DC = 8
NEG = -1.0e9
EPS = 1e-5
SCALE = 1.0 / 32.0

N_SUB = 4
SEQ = ["a", "f", "b", "f"]

Z = ("mm", 0)


class Prog:
    ENGS = ("tensor", "vector", "scalar", "gpsimd", "sync")

    def __init__(self, nc, stack):
        self.nc = nc
        self.stack = stack
        self.items = {e: [] for e in self.ENGS}
        self.sems = {}
        self.counts = {}
        for name in ("mm", "ve", "sc", "gp"):
            self.sems[name] = stack.enter_context(nc.semaphore(name))
            self.counts[name] = 0

    def chan(self, name, eng):
        return Chan(self, name, eng)

    def op(self, eng, emit, waits=(), inc=None, n=1):
        tok = None
        if inc is not None:
            self.counts[inc] += n
            tok = (inc, self.counts[inc])
        ws = tuple((s, v) for (s, v) in waits if v > 0)
        self.items[eng].append((emit, ws, inc, n))
        return tok


    def emit_all(self, block):
        for eng_name in self.ENGS:
            seq = self.items[eng_name]
            if not seq:
                continue
            sems = self.sems

            def body(eng, seq=seq):
                last = {}
                for emit, ws, inc, n in seq:
                    for s, v in ws:
                        if last.get(s, 0) < v:
                            eng.wait_ge(sems[s], v)
                            last[s] = v
                    instr = emit(eng)
                    if inc is not None:
                        instr.then_inc(sems[inc], n)
            getattr(block, eng_name)(body)


class Chan:

    def __init__(self, P, name, eng):
        self.P = P
        self.eng = eng
        self.name = name
        P.sems[name] = P.stack.enter_context(P.nc.semaphore(name))
        P.counts[name] = 0
        self.last = (name, 0)

    def dma(self, out, in_, waits=()):
        tok = self.P.op(self.eng, lambda e, o=out, i=in_: e.dma_start(out=o, in_=i),
                        waits=list(waits) + [self.last], inc=self.name, n=16)
        self.last = tok
        return tok


def build_program(n_sub=N_SUB):
    nc = bass.Bass("TRN2", target_bir_lowering=False)
    dt = nc.dram_tensor

    x_in = dt("x_in", [S, D], F32, kind="ExternalInput").ap()
    W_a = dt("W_a", [D, 3 * D], F32, kind="ExternalInput").ap()
    WqT = dt("WqT", [D, D], F32, kind="ExternalInput").ap()
    WkT = dt("WkT", [D, D], F32, kind="ExternalInput").ap()
    WvT = dt("WvT", [D, D], F32, kind="ExternalInput").ap()
    WpT = dt("WpT", [D, D], F32, kind="ExternalInput").ap()
    bq_c = dt("bq_c", [128, 8], F32, kind="ExternalInput").ap()
    bk_c = dt("bk_c", [128, 8], F32, kind="ExternalInput").ap()
    bv_row = dt("bv_row", [1, D], F32, kind="ExternalInput").ap()
    bp_row = dt("bp_row", [1, D], F32, kind="ExternalInput").ap()
    gateWT = dt("gateWT", [D, 8], F32, kind="ExternalInput").ap()
    gb_row = dt("gb_row", [1, 8], F32, kind="ExternalInput").ap()
    wiT = dt("wiT", [D, FF], F32, kind="ExternalInput").ap()
    wib_c = dt("wib_c", [128, 32], F32, kind="ExternalInput").ap()
    woT = dt("woT", [FF, D], F32, kind="ExternalInput").ap()
    wob_row = dt("wob_row", [1, D], F32, kind="ExternalInput").ap()
    AallT = dt("AallT", [D, 32], F32, kind="ExternalInput").ap()
    BcatT = dt("BcatT", [32, FF], F32, kind="ExternalInput").ap()
    maskA_h = dt("maskA_h", [128, 896], BF16, kind="ExternalInput").ap()
    maskB_h = dt("maskB_h", [128, 896], BF16, kind="ExternalInput").ap()
    ident_h = dt("ident_h", [128, 128], BF16, kind="ExternalInput").ap()

    y = dt("y", [S, D], F32, kind="ExternalOutput").ap()
    xbufA = dt("xbufA", [S, D], F32).ap()
    xbufB = dt("xbufB", [S, D], F32).ap()
    wi_sc = dt("wi_sc", [32, 128, 1024], BF16).ap()
    wo_sc = dt("wo_sc", [32, 128, D], BF16).ap()

    with ExitStack() as stk:
        E = stk.enter_context
        P = Prog(nc, stk)
        ch_x = P.chan("cx", "sync")
        ch_xr = P.chan("cr", "sync")
        ch_st = P.chan("cs", "sync")
        ch_w = P.chan("cw", "gpsimd")
        ch_wi = [P.chan("cw0", "sync"), P.chan("cw1", "sync"),
                 P.chan("cw2", "sync")]
        ch_cvg = P.chan("cvg", "gpsimd")
        ch_cvc = P.chan("cvc", "scalar")
        ch_ms = P.chan("cms", "sync")

        sc_b = [E(nc.psum_tensor(f"ps_sc{j}", [128, 512], F32)) for j in range(4)]
        av_b = [E(nc.psum_tensor(f"ps_av{j}", [128, 512], F32)) for j in range(2)]
        tp_b = [E(nc.psum_tensor(f"ps_tp{j}", [128, 1024], BF16)) for j in range(2)]

        xt = E(nc.sbuf_tensor("xt", [128, 2 * D], F32))
        xr = E(nc.sbuf_tensor("xr", [128, 2 * D], F32))
        xnew = E(nc.sbuf_tensor("xnew", [128, 2 * D], F32))
        xlnb = E(nc.sbuf_tensor("xlnb", [128, 4 * D], BF16))
        stat = E(nc.sbuf_tensor("stat", [128, 32], F32))
        smal = E(nc.sbuf_tensor("smal", [128, 32], F32))
        redu = E(nc.sbuf_tensor("redu", [128, 16], F32))
        masks = E(nc.sbuf_tensor("masks", [128, 896], BF16))
        ident = E(nc.sbuf_tensor("ident", [128, 128], BF16))
        ones1 = E(nc.sbuf_tensor("ones1", [1, 128], F32))
        row_s = E(nc.sbuf_tensor("row_s", [1, 512], F32))
        bqs = E(nc.sbuf_tensor("bqs", [128, 8], F32))
        bks = E(nc.sbuf_tensor("bks", [128, 8], F32))
        wibs = E(nc.sbuf_tensor("wibs", [128, 32], F32))

        epsc = E(nc.sbuf_tensor("epsc", [128, 1], F32))
        zerc = E(nc.sbuf_tensor("zerc", [128, 1], F32))
        cm20 = E(nc.sbuf_tensor("cm20", [128, 1], F32))
        ones_bf = E(nc.sbuf_tensor("ones_bf", [128, 1], BF16))
        identf = E(nc.sbuf_tensor("identf", [1, 1], F32))
        for dst, src in [(ident, ident_h), (bqs, bq_c),
                         (bks, bk_c), (wibs, wib_c)]:
            const_tok = ch_ms.dma(dst[:, :], src)
        mask_tok = [Z]
        P.op("vector", lambda e: e.memset(ones1[:, :], 1.0), inc="ve")
        P.op("vector", lambda e: e.memset(epsc[:, :], EPS), inc="ve")
        P.op("vector", lambda e: e.memset(zerc[:, :], 0.0), inc="ve")
        P.op("vector", lambda e: e.memset(cm20[:, :], -20.0), inc="ve")
        P.op("vector", lambda e: e.memset(identf[:, :], 1.0), inc="ve")
        P.op("vector", lambda e: e.memset(ones_bf[:, :], 1.0), inc="ve")

        xt_free = [Z, Z]
        xr_free = [Z, Z]
        xnew_free = [Z, Z]
        xlnb_free = [Z, Z, Z, Z]
        xb_free = [Z, Z]
        ao_free = [Z, Z]
        sc_free = [Z, Z, Z, Z]
        av_free = [Z, Z]
        tp_free = [Z, Z]
        store_tok = [Z] * NT
        xlnT_cur = [None]
        prelude_done = [False]
        prelude_info = [None]

        def maybe_prelude(cur_store):
            if prelude_info[0] is None or prelude_done[0]:
                return
            nxt = prelude_info[0]
            for i in range(4):
                ln_chain(i, nxt, stok=cur_store[i])
            prelude_done[0] = True

        conv_state = {"done": None}
        ch_cv = [P.chan(f"cv{j}", "gpsimd") for j in range(4)]

        def _conv_steps():
            steps = []
            for ft in range(32):
                def go(ft=ft):
                    ch_cv[ft % 4].dma(
                        wi_sc[ft, :, :].rearrange("p (c f) -> p c f", c=DC),
                        wiT[:, ft * 128:(ft + 1) * 128].rearrange("(c p) f -> p c f", p=128))
                steps.append(go)
            for fc in range(32):
                def go(fc=fc):
                    ch_cv[fc % 4].dma(wo_sc[fc, :, :], woT[fc * 128:(fc + 1) * 128, :])
                steps.append(go)

            def fin():
                conv_state["done"] = ch_cv[0].last
                conv_state["done2"] = ch_cv[1].last
                conv_state["done3"] = ch_cv[2].last
                conv_state["done4"] = ch_cv[3].last
            steps.append(fin)
            return steps

        conv_steps = _conv_steps()

        def pop_conv(k):
            for _ in range(min(k, len(conv_steps))):
                conv_steps.pop(0)()

        ln_t5 = [Z] * NT

        def ln_chain(i, x_src, stok=None):
            sl = i % 2
            xtv = xt[:, sl * D:(sl + 1) * D]
            tok_x = ch_x.dma(xtv, x_src[i * 128:(i + 1) * 128, :],
                             waits=[xt_free[sl], store_tok[i] if stok is None else stok])
            stv = stat[:, sl * 16:sl * 16 + 12]
            mv = smal[:, sl * 16:sl * 16 + 2]
            stdv = smal[:, sl * 16 + 2:sl * 16 + 3]
            rstd = smal[:, sl * 16 + 3:sl * 16 + 4]
            P.op("vector", lambda e, a=stv, b=xtv: e.bn_stats(a[:, 0:6], b[:, 0:512]),
                 waits=[tok_x], inc="ve")
            t1 = P.op("vector", lambda e, a=stv, b=xtv: e.bn_stats(a[:, 6:12], b[:, 512:1024]),
                      inc="ve")
            t2 = P.op("vector", lambda e, a=mv, b=stv: e.bn_aggr(a, b),
                      waits=[t1], inc="ve")
            t3 = P.op("scalar", lambda e, a=stdv, b=mv: e.activation(
                out=a, in_=b[:, 1:2], func=AF.Sqrt, bias=epsc[:, :], scale=1.0),
                waits=[t2], inc="sc")
            t4 = P.op("vector", lambda e, a=rstd, b=stdv: e.reciprocal(a, b),
                      waits=[t3], inc="ve")
            xlv = xlnb[:, (i % 4) * D:(i % 4 + 1) * D]
            t5 = P.op("vector", lambda e, o=xlv, a=xtv, m=mv, r=rstd: e.tensor_scalar(
                out=o, in0=a, scalar1=m[:, 0:1], scalar2=r,
                op0=OP.subtract, op1=OP.mult),
                waits=[t4, xlnb_free[i % 4]], inc="ve")
            xt_free[sl] = t5
            ln_t5[i] = t5
            return t5

        def ln_trans(i):
            g = i % 2
            xlv = xlnb[:, (i % 4) * D:(i % 4 + 1) * D]
            mt = None
            for dc in range(DC):
                mt = P.op("tensor", lambda e, d=dc, o=tp_b[g], s=xlv:
                          e.transpose(o[:, d * 128:(d + 1) * 128],
                                      s[:, d * 128:(d + 1) * 128], ident[:, :]),
                          waits=([ln_t5[i], tp_free[g], const_tok] if dc == 0 else ()), inc="mm")
            xlnb_free[i % 4] = mt
            xv = xlnT_cur[0][:, :].rearrange("p (c t) -> p c t", c=DC)[:, :, i * 128:(i + 1) * 128]
            tv = tp_b[g][:, :].rearrange("p (c t) -> p c t", c=DC)
            te = P.op("scalar", lambda e, o=xv, s=tv: e.copy(out=o, in_=s),
                      waits=[mt], inc="sc")
            tp_free[g] = te
            return te

        row_free = [Z]
        sums_free = [Z]

        def rank1_bcast(row_ap, n, out_sb):
            tok = None
            for h in range((n + 511) // 512):
                w = min(512, n - h * 512)
                b = av_b[h % 2]
                tok_r = ch_ms.dma(row_s[:, 0:w], row_ap[0:1, h * 512:h * 512 + w],
                                  waits=[row_free[0]])
                tm = P.op("tensor", lambda e, bb=b, ww=w: e.matmul(
                    bb[:, 0:ww], ones1[:, :], row_s[:, 0:ww],
                    start=True, stop=True),
                    waits=[tok_r, av_free[h % 2], ("ve", 1)], inc="mm")
                row_free[0] = tm
                tok = P.op("vector", lambda e, bb=b, hh=h, ww=w, o=out_sb: e.tensor_copy(
                    out=o[:, hh * 512:hh * 512 + ww], in_=bb[:, 0:ww]),
                    waits=[tm], inc="ve")
                av_free[h % 2] = tok
            return tok

        def emit_attn(sub, x_src, x_dst, variant, new_store):
            nsub = f"s{sub}"
            with ExitStack() as ph:
                Ep = ph.enter_context
                qT = Ep(nc.sbuf_tensor(f"qT_{nsub}", [128, 16384], BF16))
                kT = Ep(nc.sbuf_tensor(f"kT_{nsub}", [128, 16384], BF16))
                vtm = Ep(nc.sbuf_tensor(f"v_{nsub}", [128, 16384], BF16))
                wbuf = [Ep(nc.sbuf_tensor(f"wb0_{nsub}", [128, 8192], BF16))]
                if variant == "b":
                    bvb = Ep(nc.sbuf_tensor(f"bvb_{nsub}", [128, D], BF16))
                    bpb = Ep(nc.sbuf_tensor(f"bpb_{nsub}", [128, D], BF16))
                    ao_bf = Ep(nc.sbuf_tensor(f"ao_{nsub}", [128, D], BF16))
                    xb = Ep(nc.sbuf_tensor(f"xb_{nsub}", [128, D], F32))
                    bias_toks = {}

                def load_w(src_ap, col0, waits=()):
                    toks = []
                    for hh in range(2):
                        out_ap = wbuf[0][:, :].rearrange("p (c f) -> p c f", c=DC)[:, :, hh * 512:(hh + 1) * 512]
                        in_ap = src_ap[:, col0 + hh * 512:col0 + (hh + 1) * 512].rearrange(
                            "(c p) f -> p c f", p=128)
                        toks.append(ch_w.dma(out_ap, in_ap, waits=waits))
                    return toks

                xln_tok = [Z] * NT
                prod_last = Z
                bqs_s = [None]
                if variant == "b":
                    bsc = Ep(nc.sbuf_tensor(f"bqs_s_{nsub}", [128, 8], F32))
                    P.op("vector", lambda e: e.tensor_scalar(
                        out=bsc[:, :], in0=bqs[:, :], scalar1=SCALE, scalar2=None,
                        op0=OP.mult), waits=[const_tok], inc="ve")
                    bqs_s[0] = bsc

                def produce_fm(dst, wtok, bias, c, scale=1.0):
                    nonlocal prod_last
                    for et in range(8):
                        jb = et % 4
                        bank = sc_b[jb]
                        bw = [sc_free[jb], xln_tok[4 * c + 3], wtok[et // 4]]
                        mt = None
                        for dcc in range(DC):
                            mt = P.op("tensor", lambda e, b=bank, d=dcc, ee=et, cc=c, xl=xlnT_cur[0]:
                                      e.matmul(b[:, :],
                                               wbuf[0][:, d * D + ee * 128:d * D + (ee + 1) * 128],
                                               xl[:, d * 2048 + cc * 512:d * 2048 + (cc + 1) * 512],
                                               start=(d == 0), stop=(d == DC - 1)),
                                      waits=(bw if dcc == 0 else ()), inc="mm")
                        if bias is None and scale == 1.0:
                            tev = P.op("scalar", lambda e, b=bank, o=dst, ee=et, cc=c:
                                       e.copy(out=o[:, ee * 2048 + cc * 512:ee * 2048 + (cc + 1) * 512],
                                              in_=b[:, :]),
                                       waits=[mt], inc="sc")
                        else:
                            bi = zerc if bias is None else bias
                            tev = P.op("scalar", lambda e, b=bank, o=dst, ee=et, cc=c, bi=bi, sc2=scale:
                                       e.activation(out=o[:, ee * 2048 + cc * 512:ee * 2048 + (cc + 1) * 512],
                                                    in_=b[:, :], func=AF.Identity,
                                                    bias=bi[:, ee:ee + 1] if bi is not zerc else bi[:, 0:1],
                                                    scale=sc2),
                                       waits=[mt, const_tok], inc="sc")
                        sc_free[jb] = tev
                        prod_last = tev

                v_last = Z
                with nc.sbuf_tensor(f"xlnT_{nsub}", [128, DC * 2048], BF16) as xlnT:
                    xlnT_cur[0] = xlnT
                    wq_t = load_w(W_a if variant == "a" else WqT, 0)
                    if sub == 0:
                        for i in range(4):
                            ln_chain(i, x_src)
                    for i in range(4):
                        xln_tok[i] = ln_trans(i)
                    for c in range(NC):
                        if c + 1 < NC:
                            for i in range(4 * c + 4, 4 * c + 8):
                                ln_chain(i, x_src)
                        pop_conv(4)
                        produce_fm(qT, wq_t, None if variant == "a" else bqs_s[0], c,
                                   scale=SCALE)
                        if c == 0 and variant == "b":
                            bias_toks["bv"] = rank1_bcast(bv_row, D, bvb)
                            bias_toks["bp"] = rank1_bcast(bp_row, D, bpb)
                        if c + 1 < NC:
                            for i in range(4 * c + 4, 4 * c + 8):
                                xln_tok[i] = ln_trans(i)
                    wk_t = load_w(W_a if variant == "a" else WkT,
                                  D if variant == "a" else 0, waits=[prod_last])
                    for c in range(NC):
                        produce_fm(kT, wk_t, None if variant == "a" else bks, c)
                    wv_t = load_w(W_a if variant == "a" else WvT,
                                  2 * D if variant == "a" else 0, waits=[prod_last])
                    for i in range(NT):
                        pop_conv(2)
                        for dh in range(2):
                            bank = av_b[dh]
                            bw = [av_free[dh], xln_tok[i], wv_t[dh]]
                            mt = None
                            for dcc in range(DC):
                                mt = P.op("tensor", lambda e, b=bank, d=dcc, ii=i, h=dh, xl=xlnT_cur[0]:
                                          e.matmul(b[:, :],
                                                   xl[:, d * 2048 + ii * 128:d * 2048 + (ii + 1) * 128],
                                                   wbuf[0][:, d * D + h * 512:d * D + (h + 1) * 512],
                                                   start=(d == 0), stop=(d == DC - 1)),
                                          waits=(bw if dcc == 0 else ()), inc="mm")
                            if variant == "a":
                                tev = P.op("scalar", lambda e, b=bank, ii=i, h=dh:
                                           e.copy(out=vtm[:, ii * D + h * 512:ii * D + (h + 1) * 512],
                                                  in_=b[:, :]),
                                           waits=[mt], inc="sc")
                            else:
                                tev = P.op("vector", lambda e, b=bank, ii=i, h=dh:
                                           e.tensor_tensor(out=vtm[:, ii * D + h * 512:ii * D + (h + 1) * 512],
                                                           in0=b[:, :], in1=bvb[:, h * 512:(h + 1) * 512],
                                                           op=OP.add),
                                           waits=[mt, bias_toks["bv"]], inc="ve")
                            av_free[dh] = tev
                            v_last = tev

                aoT = Ep(nc.sbuf_tensor(f"aoT_{nsub}", [128, DC * 2048], BF16)) if variant == "b" else None

                if variant == "b":
                    wp_t = load_w(WpT, 0, waits=[("mm", P.counts["mm"])])

                mask_tok[0] = ch_ms.dma(masks[:, :], maskB_h if variant == "a" else maskA_h)
                mask_sb = masks
                nslot = 2 if variant == "a" else 1
                aeT = Ep(nc.sbuf_tensor(f"aeT_{nsub}", [128, nslot * 8192], BF16))
                sums_sb = row_s
                exp_tok = [Z, Z, Z]
                ep_tok = [av_free[0], av_free[1]]
                at_evac = [tp_free[0], tp_free[1]]
                aoT_last = Z
                aoT_tok = [Z] * NT
                avm_hist = [Z]

                def attn_scores_chunk(qc):
                    pop_conv(4)
                    kts = list(range(4 * qc, NT)) if variant == "a" else list(range(0, 4 * qc + 4))
                    base = (qc % 2) * 8192 if variant == "a" else 0
                    texp = Z
                    pend_cs = None
                    for kc in kts:
                        bank = sc_b[kc % 2]
                        bw = [exp_tok[kc % 2], sc_free[kc % 2], prod_last, v_last]
                        mt = None
                        for dcc in range(DC):
                            mt = P.op("tensor", lambda e, b=bank, d=dcc, kk=kc, q2=qc:
                                      e.matmul(b[:, :],
                                               kT[:, d * 2048 + kk * 128:d * 2048 + (kk + 1) * 128],
                                               qT[:, d * 2048 + q2 * 512:d * 2048 + (q2 + 1) * 512],
                                               start=(d == 0), stop=(d == DC - 1)),
                                      waits=(bw if dcc == 0 else ()), inc="mm")
                        if pend_cs is not None:
                            kcp, tep = pend_cs
                            csb = 2 + qc % 2
                            w = [tep] + ([sc_free[csb]] if kcp == kts[0] else [])
                            P.op("tensor", lambda e, kk=kcp, bs=base, b2=csb:
                                 e.matmul(sc_b[b2][0:1, 0:512], ones_bf[:, :],
                                          aeT[:, bs + kk * 512:bs + (kk + 1) * 512],
                                          start=(kk == kts[0]), stop=False),
                                 waits=w, inc="mm")
                        wmask = [mt]
                        if 4 * qc <= kc < 4 * qc + 4:
                            r = kc - 4 * qc
                            tm = P.op("vector", lambda e, b=bank, rr=r:
                                      e.tensor_tensor(out=b[:, :], in0=b[:, :],
                                                      in1=mask_sb[:, 384 - rr * 128:896 - rr * 128],
                                                      op=OP.add),
                                      waits=[mt, mask_tok[0]], inc="ve")
                            wmask = [tm]
                        texp = P.op("scalar", lambda e, b=bank, kk=kc, bs=base:
                                    e.activation(out=aeT[:, bs + kk * 512:bs + (kk + 1) * 512],
                                                 in_=b[:, :], func=AF.Exp,
                                                 bias=cm20[:, :], scale=1.0),
                                    waits=wmask + [avm_hist[0]], inc="sc")
                        exp_tok[kc % 2] = texp
                        sc_free[kc % 2] = texp
                        pend_cs = (kc, texp)
                    kcp, tep = pend_cs
                    csb = 2 + qc % 2
                    mt = P.op("tensor", lambda e, kk=kcp, bs=base, b2=csb:
                              e.matmul(sc_b[b2][0:1, 0:512], ones_bf[:, :],
                                       aeT[:, bs + kk * 512:bs + (kk + 1) * 512],
                                       start=(kk == kts[0]), stop=True),
                              waits=[tep] + ([sc_free[csb]] if kcp == kts[0] else []), inc="mm")
                    return dict(qc=qc, kts=kts, base=base, texp=texp, cs=mt)

                def attn_av_chunk(st_):
                    nonlocal aoT_last
                    qc, kts, base, texp, cs = (st_["qc"], st_["kts"], st_["base"],
                                               st_["texp"], st_["cs"])
                    r0 = 8 + (qc % 2) * 4
                    csb = 2 + qc % 2
                    tse = P.op("vector", lambda e, b2=csb: e.tensor_copy(out=sums_sb[:, :],
                                                                         in_=sc_b[b2][0:1, 0:512]),
                               waits=[cs, row_free[0]], inc="ve")
                    sc_free[csb] = tse
                    mtt = None
                    for t in range(4):
                        w = [tse, ep_tok[0]] if t == 0 else ()
                        mtt = P.op("tensor", lambda e, t2=t: e.transpose(
                            av_b[0][:, 508 + t2:508 + t2 + 1],
                            sums_sb[0:1, t2 * 128:(t2 + 1) * 128], identf[:, :]),
                            waits=w, inc="mm")
                    row_free[0] = mtt
                    trc = P.op("vector", lambda e, r2=r0: e.reciprocal(
                        redu[:, r2:r2 + 4], av_b[0][:, 508:512]),
                        waits=[mtt], inc="ve")
                    for t in range(4):
                        i = 4 * qc + t
                        avm = [None, None]
                        for n_, kc in enumerate(kts):
                            for dh in range(2):
                                bw = ()
                                if n_ == 0:
                                    bw = [ep_tok[dh], texp, v_last, trc]
                                avm[dh] = P.op("tensor", lambda e, b=av_b[dh], kk=kc, h=dh, bs=base,
                                               t2=t, first=(n_ == 0), last=(kc == kts[-1]):
                                               e.matmul(b[:, :],
                                                        aeT[:, bs + kk * 512 + t2 * 128:bs + kk * 512 + (t2 + 1) * 128],
                                                        vtm[:, kk * D + h * 512:kk * D + (h + 1) * 512],
                                                        start=first, stop=last),
                                               waits=bw, inc="mm")
                        avm_hist[0] = avm[1]
                        if variant == "a":
                            sl = i % 2
                            xrv = xr[:, sl * D:(sl + 1) * D]
                            tkr = ch_xr.dma(xrv, x_src[i * 128:(i + 1) * 128, :],
                                            waits=[xr_free[sl], store_tok[i]])
                            for dh in range(2):
                                ep_tok[dh] = P.op("vector", lambda e, b=av_b[dh], h=dh, s=sl, r2=r0 + t:
                                                  e.scalar_tensor_tensor(
                                                      out=xnew[:, s * D + h * 512:s * D + (h + 1) * 512],
                                                      in0=b[:, :], scalar=redu[:, r2:r2 + 1],
                                                      in1=xr[:, s * D + h * 512:s * D + (h + 1) * 512],
                                                      op0=OP.mult, op1=OP.add),
                                                  waits=[avm[dh], trc, tkr, xnew_free[sl]], inc="ve")
                            xr_free[sl] = ep_tok[1]
                            st = ch_st.dma(x_dst[i * 128:(i + 1) * 128, :],
                                           xnew[:, sl * D:(sl + 1) * D], waits=[ep_tok[1]])
                            xnew_free[sl] = st
                            new_store[i] = st
                            av_free[0], av_free[1] = ep_tok[0], ep_tok[1]
                        else:
                            tao = [None, None]
                            for dh in range(2):
                                tao[dh] = P.op("vector", lambda e, b=av_b[dh], h=dh, r2=r0 + t:
                                               e.tensor_scalar(
                                                   out=ao_bf[:, h * 512:(h + 1) * 512],
                                                   in0=b[:, :], scalar1=redu[:, r2:r2 + 1],
                                                   scalar2=None, op0=OP.mult),
                                               waits=[avm[dh], trc, ao_free[0]], inc="ve")
                                ep_tok[dh] = tao[dh]
                            bank = tp_b[i % 2]
                            mtr = None
                            for dcc in range(DC):
                                w = [tao[1], at_evac[i % 2]] if dcc == 0 else ()
                                mtr = P.op("tensor", lambda e, b=bank, d=dcc:
                                           e.transpose(b[:, d * 128:(d + 1) * 128],
                                                       ao_bf[:, d * 128:(d + 1) * 128],
                                                       ident[:, :]),
                                           waits=w, inc="mm")
                            aov = aoT[:, :].rearrange("p (c t) -> p c t", c=DC)[:, :, i * 128:(i + 1) * 128]
                            tpv = tp_b[i % 2][:, :].rearrange("p (c t) -> p c t", c=DC)
                            tev = P.op("vector", lambda e, o=aov, s2=tpv: e.tensor_copy(out=o, in_=s2),
                                       waits=[mtr], inc="ve")
                            at_evac[i % 2] = tev
                            ao_free[0] = ("mm", mtr[1])
                            aoT_tok[i] = tev
                            aoT_last = tev
                            av_free[0], av_free[1] = ep_tok[0], ep_tok[1]

                def proj_tiles(q0):
                    for i in range(4 * q0, 4 * q0 + 4):
                        sl = i % 2
                        xrv = xr[:, sl * D:(sl + 1) * D]
                        tkr = ch_xr.dma(xrv, x_src[i * 128:(i + 1) * 128, :],
                                        waits=[xr_free[sl], store_tok[i]])
                        tgx = P.op("gpsimd", lambda e, s=sl: e.tensor_tensor(
                            out=xb[:, :], in0=xr[:, s * D:(s + 1) * D],
                            in1=bpb[:, :], op=OP.add),
                            waits=[tkr, bias_toks["bp"], xb_free[0]], inc="gp")
                        for dh in range(2):
                            bank = av_b[dh]
                            bw = [av_free[dh], aoT_tok[i], wp_t[dh]]
                            mt = None
                            for ec in range(DC):
                                mt = P.op("tensor", lambda e, b=bank, c2=ec, ii=i, h=dh:
                                          e.matmul(b[:, :],
                                                   aoT[:, c2 * 2048 + ii * 128:c2 * 2048 + (ii + 1) * 128],
                                                   wbuf[0][:, c2 * D + h * 512:c2 * D + (h + 1) * 512],
                                                   start=(c2 == 0), stop=(c2 == DC - 1)),
                                          waits=(bw if ec == 0 else ()), inc="mm")
                            t = P.op("vector", lambda e, b=bank, h=dh, s=sl:
                                     e.tensor_tensor(
                                         out=xnew[:, s * D + h * 512:s * D + (h + 1) * 512],
                                         in0=b[:, :],
                                         in1=xb[:, h * 512:(h + 1) * 512],
                                         op=OP.add),
                                     waits=[mt, tgx, xnew_free[sl]], inc="ve")
                            av_free[dh] = t
                            ep_tok[dh] = t
                        xr_free[sl] = av_free[1]
                        xb_free[0] = av_free[1]
                        st = ch_st.dma(x_dst[i * 128:(i + 1) * 128, :],
                                       xnew[:, sl * D:(sl + 1) * D], waits=[av_free[1]])
                        xnew_free[sl] = st
                        new_store[i] = st
                        if i == 3:
                            maybe_prelude(new_store)

                if variant == "a":
                    pend = None
                    for qc in range(NC):
                        cur = attn_scores_chunk(qc)
                        if pend is not None:
                            attn_av_chunk(pend)
                            if pend["qc"] == 1:
                                maybe_prelude(new_store)
                        pend = cur
                    attn_av_chunk(pend)
                else:
                    for qc in range(NC):
                        attn_av_chunk(attn_scores_chunk(qc))
                        if qc >= 1:
                            proj_tiles(qc - 1)
                    proj_tiles(NC - 1)


                tp_free[0], tp_free[1] = at_evac[0], at_evac[1]

        def emit_ffn(sub, x_src, x_dst, new_store):
            nsub = f"s{sub}"
            pop_conv(len(conv_steps))
            with ExitStack() as ph:
                Ep = ph.enter_context
                xlnT = Ep(nc.sbuf_tensor(f"xlnT_{nsub}", [128, DC * 2048], BF16))
                xlnT_cur[0] = xlnT
                wo_sb = Ep(nc.sbuf_tensor(f"wo_{nsub}", [128, 32 * D], BF16))
                interT = Ep(nc.sbuf_tensor(f"it_{nsub}", [128, 32 * 512], BF16))
                wit = [Ep(nc.sbuf_tensor(f"wit{j}_{nsub}", [128, 2048], BF16)) for j in range(3)]
                smT = Ep(nc.sbuf_tensor(f"smT_{nsub}", [32, 2048], BF16))
                ga_bf = Ep(nc.sbuf_tensor(f"ga_{nsub}", [128, 8 * 40], BF16))
                bc_bf = Ep(nc.sbuf_tensor(f"bc_{nsub}", [32, FF], BF16))
                gbb = Ep(nc.sbuf_tensor(f"gbb_{nsub}", [128, 8], F32))
                wob = Ep(nc.sbuf_tensor(f"wob_{nsub}", [128, D], F32))
                logb = Ep(nc.sbuf_tensor(f"logb_{nsub}", [128, 2 * 8], F32))
                mke = Ep(nc.sbuf_tensor(f"mke_{nsub}", [128, 2 * 32], F32))
                sm_bf = Ep(nc.sbuf_tensor(f"smb_{nsub}", [128, 4 * 32], BF16))
                xb = Ep(nc.sbuf_tensor(f"xbf_{nsub}", [128, D], F32))

                wo_t = []
                for g in range(4):
                    out_ap = wo_sb[:, g * 8 * D:(g + 1) * 8 * D].rearrange(
                        "p (c f) -> p c f", c=8)
                    in_ap = wo_sc[g * 8:(g + 1) * 8, :, :].rearrange("c p f -> p c f")
                    wo_t.append(ch_cv[g % 4].dma(out_ap, in_ap,
                                                 waits=[conv_state["done"], conv_state["done2"],
                                                        conv_state["done3"], conv_state["done4"]]))
                tga = ch_w.dma(ga_bf[:, :].rearrange("p (c f) -> p c f", c=DC)[:, :, 0:8],
                               gateWT.rearrange("(c p) f -> p c f", p=128))
                tga = ch_w.dma(ga_bf[:, :].rearrange("p (c f) -> p c f", c=DC)[:, :, 8:40],
                               AallT.rearrange("(c p) f -> p c f", p=128), waits=[tga])
                tbc = ch_w.dma(bc_bf[:, :], BcatT)

                ep_tok = [av_free[0], av_free[1]]
                sm_free = [Z, Z, Z, Z]
                pend_tr = [[]]
                wit_freemm = [Z, Z, Z]
                twi_cur = [Z, Z, Z]
                sm_done = [Z] * NC

                xln_te = [Z] * NT

                def routing_chunk(c):
                    for i in range(4 * c, 4 * c + 4):
                        sl = i % 2
                        jb = 2 + (i % 2)
                        bank = sc_b[jb]
                        bw = [sc_free[jb], xln_te[i], tga]
                        mt = None
                        for dcc in range(DC):
                            mt = P.op("tensor", lambda e, b=bank, d=dcc, ii=i, xl=xlnT_cur[0]: e.matmul(
                                b[:, 0:40],
                                xl[:, d * 2048 + ii * 128:d * 2048 + (ii + 1) * 128],
                                ga_bf[:, d * 40:(d + 1) * 40],
                                start=(d == 0), stop=(d == DC - 1)),
                                waits=(bw if dcc == 0 else ()), inc="mm")
                        t1 = P.op("vector", lambda e, b=bank, s=sl: e.tensor_tensor(
                            out=logb[:, s * 8:(s + 1) * 8], in0=b[:, 0:8], in1=gbb[:, :],
                            op=OP.add),
                            waits=[mt, tgb[0]], inc="ve")
                        t2 = P.op("vector", lambda e, s=sl: e.reduce_max(
                            redu[:, 12:13], logb[:, s * 8:(s + 1) * 8], axis=AX.X),
                            waits=[t1], inc="ve")
                        t3 = P.op("vector", lambda e, s=sl: e.tensor_scalar(
                            out=logb[:, s * 8:(s + 1) * 8], in0=logb[:, s * 8:(s + 1) * 8],
                            scalar1=redu[:, 12:13], scalar2=None, op0=OP.is_equal),
                            waits=[t2], inc="ve")
                        def _t5(e, b=bank, s4=(i % 4), m2=sl):
                            mb = bass.AP(logb[:, :].tensor, m2 * 8, [[16, 128], [0, 4], [1, 8]])
                            return e.tensor_tensor(
                                out=sm_bf[:, s4 * 32:(s4 + 1) * 32].rearrange(
                                    "p (r e2) -> p r e2", r=4),
                                in0=b[:, 8:40].rearrange("p (r e2) -> p r e2", r=4),
                                in1=mb, op=OP.mult)
                        t5 = P.op("vector", _t5, waits=[t3, sm_free[i % 4]], inc="ve")
                        sc_free[jb] = t5
                        pend_tr[0].append((i, i % 4, t5))
                        if i % 4 == 3:
                            mt3 = None
                            for (i2, s4, t52) in pend_tr[0]:
                                mt3 = P.op("tensor", lambda e, s=s4, ii=i2: e.transpose(
                                    tp_b[1][0:32, (ii % 4) * 128:(ii % 4 + 1) * 128],
                                    sm_bf[:, s * 32:(s + 1) * 32], ident[:, :]),
                                    waits=([t52, tp_free[1]] if i2 % 4 == 0 else [t52]),
                                    inc="mm")
                                sm_free[s4] = mt3
                            pend_tr[0] = []
                            t6 = P.op("vector", lambda e, cc=c: e.tensor_copy(
                                out=smT[0:32, cc * 512:(cc + 1) * 512], in_=tp_b[1][0:32, 0:512]),
                                waits=[mt3], inc="ve")
                            tp_free[1] = t6
                            sm_done[c] = t6

                if sub == 0:
                    for i in range(4):
                        ln_chain(i, x_src)
                for i in range(4):
                    xln_te[i] = ln_trans(i)
                tgb = [Z]
                tgb[0] = rank1_bcast(gb_row, 8, gbb)
                twob = rank1_bcast(wob_row, D, wob)
                routing_chunk(0)
                for c in range(NC):
                    if c + 1 < NC:
                        for i in range(4 * c + 4, 4 * c + 8):
                            ln_chain(i, x_src)
                    xln_last = xln_te[4 * c + 3]
                    pop_conv(4)
                    it_s = Z
                    it_v = Z

                    def issue_wi(g):
                        twi_cur[g % 3] = ch_wi[g % 3].dma(
                            wit[g % 3][:, :].rearrange("p (c f) -> p c f", c=2),
                            wi_sc[2 * g:2 * g + 2, :, :].rearrange("c p f -> p c f"),
                            waits=[conv_state["done"], conv_state["done2"],
                                   conv_state["done3"], conv_state["done4"],
                                   wit_freemm[g % 3]])

                    for g in range(2):
                        issue_wi(g)
                    for ft in range(32):
                        jb = ft % 4
                        bank = sc_b[jb]
                        if ft % 2 == 0 and ft // 2 + 2 < 16:
                            issue_wi(ft // 2 + 2)
                        g = ft // 2
                        twi = twi_cur[g % 3]
                        bw = [sc_free[jb], twi, sm_done[c], xln_last, tbc]
                        mt = None
                        for dcc in range(DC):
                            mt = P.op("tensor", lambda e, b=bank, d=dcc, cc=c, f=ft, xl=xlnT_cur[0]: e.matmul(
                                b[:, :], wit[(f // 2) % 3][:, (f % 2) * 1024 + d * 128:(f % 2) * 1024 + (d + 1) * 128],
                                xl[:, d * 2048 + cc * 512:d * 2048 + (cc + 1) * 512],
                                start=(d == 0), stop=False),
                                waits=(bw if dcc == 0 else ()), inc="mm")
                        mt = P.op("tensor", lambda e, b=bank, cc=c, f=ft: e.matmul(
                            b[:, :], bc_bf[0:32, f * 128:(f + 1) * 128],
                            smT[0:32, cc * 512:(cc + 1) * 512],
                            start=False, stop=True), inc="mm")
                        if ft % 2 == 1:
                            wit_freemm[(ft // 2) % 3] = mt
                        if ft % 2 == 0:
                            tev = P.op("scalar", lambda e, b=bank, f=ft: e.activation(
                                out=interT[:, f * 512:(f + 1) * 512], in_=b[:, :], func=AF.Relu,
                                bias=wibs[:, f:f + 1], scale=1.0),
                                waits=[mt, const_tok], inc="sc")
                            it_s = tev
                        else:
                            tev = P.op("vector", lambda e, b=bank, f=ft: e.tensor_scalar(
                                out=interT[:, f * 512:(f + 1) * 512], in0=b[:, :],
                                scalar1=wibs[:, f:f + 1], scalar2=0.0,
                                op0=OP.add, op1=OP.max),
                                waits=[mt, const_tok], inc="ve")
                            it_v = tev
                        sc_free[jb] = tev
                    for tt in range(4):
                        i = 4 * c + tt
                        sl = i % 2
                        xrv = xr[:, sl * D:(sl + 1) * D]
                        tkr = ch_xr.dma(xrv, x_src[i * 128:(i + 1) * 128, :],
                                        waits=[xr_free[sl], store_tok[i]])
                        tgx = P.op("gpsimd", lambda e, s=sl: e.tensor_tensor(
                            out=xb[:, :], in0=xr[:, s * D:(s + 1) * D],
                            in1=wob[:, :], op=OP.add),
                            waits=[tkr, twob, xb_free[0]], inc="gp")
                        for dh in range(2):
                            bank = av_b[dh]
                            bw = [av_free[dh], it_s, it_v, wo_t[0]]
                            mt = None
                            for fc in range(32):
                                mt = P.op("tensor", lambda e, b=bank, f=fc, t2=tt, h=dh: e.matmul(
                                    b[:, :],
                                    interT[:, f * 512 + t2 * 128:f * 512 + (t2 + 1) * 128],
                                    wo_sb[:, f * D + h * 512:f * D + (h + 1) * 512],
                                    start=(f == 0), stop=(f == 31)),
                                    waits=(bw if fc == 0 else [wo_t[fc // 8]]), inc="mm")
                            av_free[dh] = P.op("vector", lambda e, b=bank, h=dh, s=sl:
                                               e.tensor_tensor(
                                                   out=xnew[:, s * D + h * 512:s * D + (h + 1) * 512],
                                                   in0=b[:, :],
                                                   in1=xb[:, h * 512:(h + 1) * 512],
                                                   op=OP.add),
                                               waits=[mt, tgx, xnew_free[sl]], inc="ve")
                        xr_free[sl] = av_free[1]
                        xb_free[0] = av_free[1]
                        st = ch_st.dma(x_dst[i * 128:(i + 1) * 128, :],
                                       xnew[:, sl * D:(sl + 1) * D], waits=[av_free[1]])
                        xnew_free[sl] = st
                        new_store[i] = st
                    if c + 1 < NC:
                        for i2 in range(4 * c + 4, 4 * c + 8):
                            xln_te[i2] = ln_trans(i2)
                        routing_chunk(c + 1)
                    if c == 2:
                        maybe_prelude(new_store)

        def phase_fence():
            toks = [(s, P.counts[s]) for s in ("mm", "ve", "sc", "gp")]
            for eng in ("tensor", "vector", "scalar", "gpsimd"):
                sem = {"tensor": "mm", "vector": "ve", "scalar": "sc", "gpsimd": "gp"}[eng]
                P.op(eng, lambda e: e.nop(),
                     waits=[t for t in toks if t[0] != sem])

        x_src = x_in
        for sub in range(n_sub):
            x_dst = y if sub == n_sub - 1 else (xbufA if sub % 2 == 0 else xbufB)
            kind = SEQ[sub]
            new_store = [Z] * NT
            prelude_info[0] = x_dst if sub != n_sub - 1 else None
            if kind == "f":
                emit_ffn(sub, x_src, x_dst, new_store)
            else:
                emit_attn(sub, x_src, x_dst, kind, new_store)
            store_tok[:] = new_store
            x_src = x_dst
            if sub != n_sub - 1:
                if not prelude_done[0]:
                    for i in range(4):
                        ln_chain(i, x_src)
                phase_fence()
            prelude_done[0] = False
        pop_conv(len(conv_steps))

        P.op("sync", lambda e: e.nop(),
             waits=[ch.last for ch in (ch_x, ch_xr, ch_st, ch_w, ch_wi[0], ch_wi[1],
                                       ch_wi[2], ch_cv[0], ch_cv[1], ch_cv[2],
                                       ch_cv[3], ch_ms)])

        with nc.Block() as block:
            P.emit_all(block)

    return nc


def _prep(inputs):
    f32 = np.float32
    c = np.ascontiguousarray
    d = {}
    d["W_a"] = c(np.asarray(inputs["L0_attn_a_W"], f32))
    d["WqT"] = c(np.asarray(inputs["L0_attn_b_Wq"], f32).T)
    d["WkT"] = c(np.asarray(inputs["L0_attn_b_Wk"], f32).T)
    d["WvT"] = c(np.asarray(inputs["L0_attn_b_Wv"], f32).T)
    d["WpT"] = c(np.asarray(inputs["L0_attn_b_Wp"], f32).T)
    d["bq_c"] = c(np.asarray(inputs["L0_attn_b_bq"], f32).reshape(8, 128).T)
    d["bk_c"] = c(np.asarray(inputs["L0_attn_b_bk"], f32).reshape(8, 128).T)
    d["bv_row"] = c(np.asarray(inputs["L0_attn_b_bv"], f32).reshape(1, D))
    d["bp_row"] = c(np.asarray(inputs["L0_attn_b_bp"], f32).reshape(1, D))
    d["gateWT"] = c(np.asarray(inputs["L0_ffn_gate_W"], f32).T)
    d["gb_row"] = c(np.asarray(inputs["L0_ffn_gate_b"], f32).reshape(1, 8))
    d["wiT"] = c(np.asarray(inputs["L0_ffn_wi_W"], f32).T)
    d["wib_c"] = c(np.asarray(inputs["L0_ffn_wi_b"], f32).reshape(32, 128).T)
    d["woT"] = c(np.asarray(inputs["L0_ffn_wo_W"], f32).T)
    d["wob_row"] = c(np.asarray(inputs["L0_ffn_wo_b"], f32).reshape(1, D))
    d["AallT"] = c(np.asarray(inputs["L0_ffn_lora_A"], f32).transpose(1, 0, 2).reshape(32, D).T)
    d["BcatT"] = c(np.asarray(inputs["L0_ffn_lora_B"], f32).transpose(2, 0, 1).reshape(32, FF))
    p = np.arange(128)[:, None]
    u = np.arange(896)[None, :]
    d["maskA_h"] = np.where(u < p + 384, NEG, 0.0).astype(ml_dtypes.bfloat16)
    d["maskB_h"] = np.where(u > p + 384, NEG, 0.0).astype(ml_dtypes.bfloat16)
    d["ident_h"] = np.eye(128, dtype=ml_dtypes.bfloat16)
    return d


def kernel(**inputs):
    shared = _prep(inputs)
    x = np.asarray(inputs["L0_attn_a_x"], np.float32)
    nc = build_program(N_SUB)
    in_maps = [dict(shared, x_in=np.ascontiguousarray(x[c])) for c in range(8)]
    res = run_bass_kernel_spmd(nc, in_maps, core_ids=list(range(8)))
    return np.stack([np.asarray(res.results[c]["y"]) for c in range(8)], axis=0)


# version 26
# speedup vs baseline: 1.1687x; 1.1687x over previous
import numpy as np
import ml_dtypes
from contextlib import ExitStack

import concourse.bass as bass
import concourse.mybir as mybir
from concourse.bass_utils import run_bass_kernel_spmd
from concourse.alu_op_type import AluOpType as OP

F32 = mybir.dt.float32
BF16 = mybir.dt.bfloat16
AF = mybir.ActivationFunctionType
AX = mybir.AxisListType

S = 2048
D = 1024
FF = 4096
NT = 16
NC = 4
DC = 8
NEG = -1.0e9
EPS = 1e-5
SCALE = 1.0 / 32.0

N_SUB = 4
SEQ = ["a", "f", "b", "f"]

Z = ("mm", 0)


class Prog:
    ENGS = ("tensor", "vector", "scalar", "gpsimd", "sync")

    def __init__(self, nc, stack):
        self.nc = nc
        self.stack = stack
        self.items = {e: [] for e in self.ENGS}
        self.sems = {}
        self.counts = {}
        for name in ("mm", "ve", "sc", "gp"):
            self.sems[name] = stack.enter_context(nc.semaphore(name))
            self.counts[name] = 0

    def chan(self, name, eng):
        return Chan(self, name, eng)

    def op(self, eng, emit, waits=(), inc=None, n=1):
        tok = None
        if inc is not None:
            self.counts[inc] += n
            tok = (inc, self.counts[inc])
        ws = tuple((s, v) for (s, v) in waits if v > 0)
        self.items[eng].append((emit, ws, inc, n))
        return tok


    def emit_all(self, block):
        for eng_name in self.ENGS:
            seq = self.items[eng_name]
            if not seq:
                continue
            sems = self.sems

            def body(eng, seq=seq):
                last = {}
                for emit, ws, inc, n in seq:
                    for s, v in ws:
                        if last.get(s, 0) < v:
                            eng.wait_ge(sems[s], v)
                            last[s] = v
                    instr = emit(eng)
                    if inc is not None:
                        instr.then_inc(sems[inc], n)
            getattr(block, eng_name)(body)


class Chan:

    def __init__(self, P, name, eng):
        self.P = P
        self.eng = eng
        self.name = name
        P.sems[name] = P.stack.enter_context(P.nc.semaphore(name))
        P.counts[name] = 0
        self.last = (name, 0)

    def dma(self, out, in_, waits=()):
        tok = self.P.op(self.eng, lambda e, o=out, i=in_: e.dma_start(out=o, in_=i),
                        waits=list(waits) + [self.last], inc=self.name, n=16)
        self.last = tok
        return tok


def build_program(n_sub=N_SUB):
    nc = bass.Bass("TRN2", target_bir_lowering=False)
    dt = nc.dram_tensor

    x_in = dt("x_in", [S, D], F32, kind="ExternalInput").ap()
    W_a = dt("W_a", [D, 3 * D], F32, kind="ExternalInput").ap()
    WqT = dt("WqT", [D, D], F32, kind="ExternalInput").ap()
    WkT = dt("WkT", [D, D], F32, kind="ExternalInput").ap()
    WvT = dt("WvT", [D, D], F32, kind="ExternalInput").ap()
    WpT = dt("WpT", [D, D], F32, kind="ExternalInput").ap()
    bq_c = dt("bq_c", [128, 8], F32, kind="ExternalInput").ap()
    bk_c = dt("bk_c", [128, 8], F32, kind="ExternalInput").ap()
    bv_row = dt("bv_row", [1, D], F32, kind="ExternalInput").ap()
    bp_row = dt("bp_row", [1, D], F32, kind="ExternalInput").ap()
    gateWT = dt("gateWT", [D, 8], F32, kind="ExternalInput").ap()
    gb_row = dt("gb_row", [1, 8], F32, kind="ExternalInput").ap()
    wiT = dt("wiT", [D, FF], F32, kind="ExternalInput").ap()
    wib_c = dt("wib_c", [128, 32], F32, kind="ExternalInput").ap()
    woT = dt("woT", [FF, D], F32, kind="ExternalInput").ap()
    wob_row = dt("wob_row", [1, D], F32, kind="ExternalInput").ap()
    AallT = dt("AallT", [D, 32], F32, kind="ExternalInput").ap()
    BcatT = dt("BcatT", [32, FF], F32, kind="ExternalInput").ap()
    maskA_h = dt("maskA_h", [128, 896], BF16, kind="ExternalInput").ap()
    maskB_h = dt("maskB_h", [128, 896], BF16, kind="ExternalInput").ap()
    ident_h = dt("ident_h", [128, 128], BF16, kind="ExternalInput").ap()

    y = dt("y", [S, D], F32, kind="ExternalOutput").ap()
    xbufA = dt("xbufA", [S, D], F32).ap()
    xbufB = dt("xbufB", [S, D], F32).ap()
    wi_sc = dt("wi_sc", [32, 128, 1024], BF16).ap()
    wo_sc = dt("wo_sc", [32, 128, D], BF16).ap()

    with ExitStack() as stk:
        E = stk.enter_context
        P = Prog(nc, stk)
        ch_x = P.chan("cx", "sync")
        ch_xr = P.chan("cr", "sync")
        ch_st = P.chan("cs", "sync")
        ch_w = P.chan("cw", "gpsimd")
        ch_wi = [P.chan("cw0", "sync"), P.chan("cw1", "sync"),
                 P.chan("cw2", "sync")]
        ch_cvg = P.chan("cvg", "gpsimd")
        ch_cvc = P.chan("cvc", "scalar")
        ch_ms = P.chan("cms", "sync")

        sc_b = [E(nc.psum_tensor(f"ps_sc{j}", [128, 512], F32)) for j in range(4)]
        av_b = [E(nc.psum_tensor(f"ps_av{j}", [128, 512], F32)) for j in range(2)]
        tp_b = [E(nc.psum_tensor(f"ps_tp{j}", [128, 1024], BF16)) for j in range(2)]

        xt = E(nc.sbuf_tensor("xt", [128, 2 * D], F32))
        xr = E(nc.sbuf_tensor("xr", [128, 2 * D], F32))
        xnew = E(nc.sbuf_tensor("xnew", [128, 2 * D], F32))
        xlnb = E(nc.sbuf_tensor("xlnb", [128, 4 * D], BF16))
        stat = E(nc.sbuf_tensor("stat", [128, 32], F32))
        smal = E(nc.sbuf_tensor("smal", [128, 32], F32))
        redu = E(nc.sbuf_tensor("redu", [128, 16], F32))
        masks = E(nc.sbuf_tensor("masks", [128, 896], BF16))
        ident = E(nc.sbuf_tensor("ident", [128, 128], BF16))
        ones1 = E(nc.sbuf_tensor("ones1", [1, 128], F32))
        row_s = E(nc.sbuf_tensor("row_s", [1, 512], F32))
        bqs = E(nc.sbuf_tensor("bqs", [128, 8], F32))
        bks = E(nc.sbuf_tensor("bks", [128, 8], F32))
        wibs = E(nc.sbuf_tensor("wibs", [128, 32], F32))

        epsc = E(nc.sbuf_tensor("epsc", [128, 1], F32))
        zerc = E(nc.sbuf_tensor("zerc", [128, 1], F32))
        cm20 = E(nc.sbuf_tensor("cm20", [128, 1], F32))
        ones_bf = E(nc.sbuf_tensor("ones_bf", [128, 1], BF16))
        identf = E(nc.sbuf_tensor("identf", [1, 1], F32))
        for dst, src in [(ident, ident_h), (bqs, bq_c),
                         (bks, bk_c), (wibs, wib_c)]:
            const_tok = ch_ms.dma(dst[:, :], src)
        mask_tok = [Z]
        P.op("vector", lambda e: e.memset(ones1[:, :], 1.0), inc="ve")
        P.op("vector", lambda e: e.memset(epsc[:, :], EPS), inc="ve")
        P.op("vector", lambda e: e.memset(zerc[:, :], 0.0), inc="ve")
        P.op("vector", lambda e: e.memset(cm20[:, :], -20.0), inc="ve")
        P.op("vector", lambda e: e.memset(identf[:, :], 1.0), inc="ve")
        P.op("vector", lambda e: e.memset(ones_bf[:, :], 1.0), inc="ve")

        xt_free = [Z, Z]
        xr_free = [Z, Z]
        xnew_free = [Z, Z]
        xlnb_free = [Z, Z, Z, Z]
        xb_free = [Z, Z]
        ao_free = [Z, Z]
        sc_free = [Z, Z, Z, Z]
        av_free = [Z, Z]
        tp_free = [Z, Z]
        store_tok = [Z] * NT
        xlnT_cur = [None]
        prelude_done = [False]
        prelude_info = [None]

        def maybe_prelude(cur_store):
            if prelude_info[0] is None or prelude_done[0]:
                return
            nxt = prelude_info[0]
            for i in range(4):
                ln_chain(i, nxt, stok=cur_store[i])
            prelude_done[0] = True

        conv_state = {"done": None}
        ch_cv = [P.chan(f"cv{j}", "gpsimd") for j in range(4)]

        def _conv_steps():
            steps = []
            for ft in range(32):
                def go(ft=ft):
                    ch_cv[ft % 4].dma(
                        wi_sc[ft, :, :].rearrange("p (c f) -> p c f", c=DC),
                        wiT[:, ft * 128:(ft + 1) * 128].rearrange("(c p) f -> p c f", p=128))
                steps.append(go)
            for fc in range(32):
                def go(fc=fc):
                    ch_cv[fc % 4].dma(wo_sc[fc, :, :], woT[fc * 128:(fc + 1) * 128, :])
                steps.append(go)

            def fin():
                conv_state["done"] = ch_cv[0].last
                conv_state["done2"] = ch_cv[1].last
                conv_state["done3"] = ch_cv[2].last
                conv_state["done4"] = ch_cv[3].last
            steps.append(fin)
            return steps

        conv_steps = _conv_steps()

        def pop_conv(k):
            for _ in range(min(k, len(conv_steps))):
                conv_steps.pop(0)()

        ln_t5 = [Z] * NT

        def ln_chain(i, x_src, stok=None):
            sl = i % 2
            xtv = xt[:, sl * D:(sl + 1) * D]
            tok_x = ch_x.dma(xtv, x_src[i * 128:(i + 1) * 128, :],
                             waits=[xt_free[sl], store_tok[i] if stok is None else stok])
            stv = stat[:, sl * 16:sl * 16 + 12]
            mv = smal[:, sl * 16:sl * 16 + 2]
            stdv = smal[:, sl * 16 + 2:sl * 16 + 3]
            rstd = smal[:, sl * 16 + 3:sl * 16 + 4]
            P.op("vector", lambda e, a=stv, b=xtv: e.bn_stats(a[:, 0:6], b[:, 0:512]),
                 waits=[tok_x], inc="ve")
            t1 = P.op("vector", lambda e, a=stv, b=xtv: e.bn_stats(a[:, 6:12], b[:, 512:1024]),
                      inc="ve")
            t2 = P.op("vector", lambda e, a=mv, b=stv: e.bn_aggr(a, b),
                      waits=[t1], inc="ve")
            t3 = P.op("scalar", lambda e, a=stdv, b=mv: e.activation(
                out=a, in_=b[:, 1:2], func=AF.Sqrt, bias=epsc[:, :], scale=1.0),
                waits=[t2], inc="sc")
            t4 = P.op("vector", lambda e, a=rstd, b=stdv: e.reciprocal(a, b),
                      waits=[t3], inc="ve")
            xlv = xlnb[:, (i % 4) * D:(i % 4 + 1) * D]
            t5 = P.op("vector", lambda e, o=xlv, a=xtv, m=mv, r=rstd: e.tensor_scalar(
                out=o, in0=a, scalar1=m[:, 0:1], scalar2=r,
                op0=OP.subtract, op1=OP.mult),
                waits=[t4, xlnb_free[i % 4]], inc="ve")
            xt_free[sl] = t5
            ln_t5[i] = t5
            return t5

        def ln_trans(i):
            g = i % 2
            xlv = xlnb[:, (i % 4) * D:(i % 4 + 1) * D]
            mt = None
            for dc in range(DC):
                mt = P.op("tensor", lambda e, d=dc, o=tp_b[g], s=xlv:
                          e.transpose(o[:, d * 128:(d + 1) * 128],
                                      s[:, d * 128:(d + 1) * 128], ident[:, :]),
                          waits=([ln_t5[i], tp_free[g], const_tok] if dc == 0 else ()), inc="mm")
            xlnb_free[i % 4] = mt
            xv = xlnT_cur[0][:, :].rearrange("p (c t) -> p c t", c=DC)[:, :, i * 128:(i + 1) * 128]
            tv = tp_b[g][:, :].rearrange("p (c t) -> p c t", c=DC)
            te = P.op("scalar", lambda e, o=xv, s=tv: e.copy(out=o, in_=s),
                      waits=[mt], inc="sc")
            tp_free[g] = te
            return te

        row_free = [Z]
        sums_free = [Z]

        def rank1_bcast(row_ap, n, out_sb):
            tok = None
            for h in range((n + 511) // 512):
                w = min(512, n - h * 512)
                b = av_b[h % 2]
                tok_r = ch_ms.dma(row_s[:, 0:w], row_ap[0:1, h * 512:h * 512 + w],
                                  waits=[row_free[0]])
                tm = P.op("tensor", lambda e, bb=b, ww=w: e.matmul(
                    bb[:, 0:ww], ones1[:, :], row_s[:, 0:ww],
                    start=True, stop=True),
                    waits=[tok_r, av_free[h % 2], ("ve", 1)], inc="mm")
                row_free[0] = tm
                tok = P.op("vector", lambda e, bb=b, hh=h, ww=w, o=out_sb: e.tensor_copy(
                    out=o[:, hh * 512:hh * 512 + ww], in_=bb[:, 0:ww]),
                    waits=[tm], inc="ve")
                av_free[h % 2] = tok
            return tok

        def emit_attn(sub, x_src, x_dst, variant, new_store):
            nsub = f"s{sub}"
            with ExitStack() as ph:
                Ep = ph.enter_context
                qT = Ep(nc.sbuf_tensor(f"qT_{nsub}", [128, 16384], BF16))
                kT = Ep(nc.sbuf_tensor(f"kT_{nsub}", [128, 16384], BF16))
                vtm = Ep(nc.sbuf_tensor(f"v_{nsub}", [128, 16384], BF16))
                wbuf = [Ep(nc.sbuf_tensor(f"wb0_{nsub}", [128, 12288 if variant == "a" else 8192], BF16))]
                if variant == "b":
                    bvb = Ep(nc.sbuf_tensor(f"bvb_{nsub}", [128, D], BF16))
                    bpb = Ep(nc.sbuf_tensor(f"bpb_{nsub}", [128, D], BF16))
                    ao_bf = Ep(nc.sbuf_tensor(f"ao_{nsub}", [128, D], BF16))
                    xb = Ep(nc.sbuf_tensor(f"xb_{nsub}", [128, D], F32))
                    bias_toks = {}

                def load_w(src_ap, col0, waits=()):
                    toks = []
                    for hh in range(2):
                        out_ap = wbuf[0][:, :].rearrange("p (c f) -> p c f", c=DC)[:, :, hh * 512:(hh + 1) * 512]
                        in_ap = src_ap[:, col0 + hh * 512:col0 + (hh + 1) * 512].rearrange(
                            "(c p) f -> p c f", p=128)
                        toks.append(ch_w.dma(out_ap, in_ap, waits=waits))
                    return toks

                def load_half(src_ap, col0, hh, slot, waits=()):
                    out_ap = wbuf[0][:, slot * 4096:(slot + 1) * 4096].rearrange(
                        "p (c f) -> p c f", c=DC)
                    in_ap = src_ap[:, col0 + hh * 512:col0 + (hh + 1) * 512].rearrange(
                        "(c p) f -> p c f", p=128)
                    return ch_w.dma(out_ap, in_ap, waits=waits)

                xln_tok = [Z] * NT
                prod_last = Z
                bqs_s = [None]
                if variant == "b":
                    bsc = Ep(nc.sbuf_tensor(f"bqs_s_{nsub}", [128, 8], F32))
                    P.op("vector", lambda e: e.tensor_scalar(
                        out=bsc[:, :], in0=bqs[:, :], scalar1=SCALE, scalar2=None,
                        op0=OP.mult), waits=[const_tok], inc="ve")
                    bqs_s[0] = bsc

                def produce_fm(dst, wtok, bias, c, scale=1.0, slots=None):
                    nonlocal prod_last
                    for et in range(8):
                        jb = et % 4
                        bank = sc_b[jb]
                        bw = [sc_free[jb], xln_tok[4 * c + 3], wtok[et // 4]]
                        if slots is None:
                            wcol = lambda d, ee: d * D + ee * 128
                        else:
                            wcol = lambda d, ee: slots[ee // 4] * 4096 + d * 512 + (ee % 4) * 128
                        mt = None
                        for dcc in range(DC):
                            mt = P.op("tensor", lambda e, b=bank, d=dcc, ee=et, cc=c, xl=xlnT_cur[0],
                                      wc=wcol(dcc, et):
                                      e.matmul(b[:, :],
                                               wbuf[0][:, wc:wc + 128],
                                               xl[:, d * 2048 + cc * 512:d * 2048 + (cc + 1) * 512],
                                               start=(d == 0), stop=(d == DC - 1)),
                                      waits=(bw if dcc == 0 else ()), inc="mm")
                        if bias is None and scale == 1.0:
                            tev = P.op("scalar", lambda e, b=bank, o=dst, ee=et, cc=c:
                                       e.copy(out=o[:, ee * 2048 + cc * 512:ee * 2048 + (cc + 1) * 512],
                                              in_=b[:, :]),
                                       waits=[mt], inc="sc")
                        else:
                            bi = zerc if bias is None else bias
                            tev = P.op("scalar", lambda e, b=bank, o=dst, ee=et, cc=c, bi=bi, sc2=scale:
                                       e.activation(out=o[:, ee * 2048 + cc * 512:ee * 2048 + (cc + 1) * 512],
                                                    in_=b[:, :], func=AF.Identity,
                                                    bias=bi[:, ee:ee + 1] if bi is not zerc else bi[:, 0:1],
                                                    scale=sc2),
                                       waits=[mt, const_tok], inc="sc")
                        sc_free[jb] = tev
                        prod_last = tev

                v_last = Z
                with nc.sbuf_tensor(f"xlnT_{nsub}", [128, DC * 2048], BF16) as xlnT:
                    xlnT_cur[0] = xlnT
                    if variant == "a":
                        wq_t = [load_half(W_a, 0, 0, 0), load_half(W_a, 0, 1, 1)]
                        tk0 = load_half(W_a, D, 0, 2)
                    else:
                        wq_t = load_w(WqT, 0)
                    if sub == 0:
                        for i in range(4):
                            ln_chain(i, x_src)
                    for i in range(4):
                        xln_tok[i] = ln_trans(i)
                    for c in range(NC):
                        if c + 1 < NC:
                            for i in range(4 * c + 4, 4 * c + 8):
                                ln_chain(i, x_src)
                        pop_conv(4)
                        produce_fm(qT, wq_t, None if variant == "a" else bqs_s[0], c,
                                   scale=SCALE,
                                   slots=(0, 1) if variant == "a" else None)
                        if c == 0 and variant == "b":
                            bias_toks["bv"] = rank1_bcast(bv_row, D, bvb)
                            bias_toks["bp"] = rank1_bcast(bp_row, D, bpb)
                        if c + 1 < NC:
                            for i in range(4 * c + 4, 4 * c + 8):
                                xln_tok[i] = ln_trans(i)
                    if variant == "a":
                        wk_t = [tk0, load_half(W_a, D, 1, 0, waits=[prod_last])]
                        tv0 = load_half(W_a, 2 * D, 0, 1, waits=[prod_last])
                    else:
                        wk_t = load_w(WkT, 0, waits=[prod_last])
                    for c in range(NC):
                        produce_fm(kT, wk_t, None if variant == "a" else bks, c,
                                   slots=(2, 0) if variant == "a" else None)
                    if variant == "a":
                        wv_t = [tv0, load_half(W_a, 2 * D, 1, 2, waits=[prod_last])]
                        v_slots = (1, 2)
                    else:
                        wv_t = load_w(WvT, 0, waits=[prod_last])
                        v_slots = None
                    for i in range(NT):
                        pop_conv(2)
                        for dh in range(2):
                            bank = av_b[dh]
                            bw = [av_free[dh], xln_tok[i], wv_t[dh]]
                            mt = None
                            for dcc in range(DC):
                                if v_slots is None:
                                    wc = dcc * D + dh * 512
                                else:
                                    wc = v_slots[dh] * 4096 + dcc * 512
                                mt = P.op("tensor", lambda e, b=bank, d=dcc, ii=i, h=dh, xl=xlnT_cur[0],
                                          wc2=wc:
                                          e.matmul(b[:, :],
                                                   xl[:, d * 2048 + ii * 128:d * 2048 + (ii + 1) * 128],
                                                   wbuf[0][:, wc2:wc2 + 512],
                                                   start=(d == 0), stop=(d == DC - 1)),
                                          waits=(bw if dcc == 0 else ()), inc="mm")
                            if variant == "a":
                                tev = P.op("scalar", lambda e, b=bank, ii=i, h=dh:
                                           e.copy(out=vtm[:, ii * D + h * 512:ii * D + (h + 1) * 512],
                                                  in_=b[:, :]),
                                           waits=[mt], inc="sc")
                            else:
                                tev = P.op("vector", lambda e, b=bank, ii=i, h=dh:
                                           e.tensor_tensor(out=vtm[:, ii * D + h * 512:ii * D + (h + 1) * 512],
                                                           in0=b[:, :], in1=bvb[:, h * 512:(h + 1) * 512],
                                                           op=OP.add),
                                           waits=[mt, bias_toks["bv"]], inc="ve")
                            av_free[dh] = tev
                            v_last = tev

                aoT = Ep(nc.sbuf_tensor(f"aoT_{nsub}", [128, DC * 2048], BF16)) if variant == "b" else None

                if variant == "b":
                    wp_t = load_w(WpT, 0, waits=[("mm", P.counts["mm"])])

                mask_tok[0] = ch_ms.dma(masks[:, :], maskB_h if variant == "a" else maskA_h)
                mask_sb = masks
                nslot = 2 if variant == "a" else 1
                aeT = Ep(nc.sbuf_tensor(f"aeT_{nsub}", [128, nslot * 8192], BF16))
                sums_sb = row_s
                exp_tok = [Z, Z, Z]
                ep_tok = [av_free[0], av_free[1]]
                at_evac = [tp_free[0], tp_free[1]]
                aoT_last = Z
                aoT_tok = [Z] * NT
                avm_hist = [Z]

                def attn_scores_chunk(qc):
                    pop_conv(4)
                    kts = list(range(4 * qc, NT)) if variant == "a" else list(range(0, 4 * qc + 4))
                    base = (qc % 2) * 8192 if variant == "a" else 0
                    texp = Z
                    pend_cs = None
                    for kc in kts:
                        bank = sc_b[kc % 2]
                        bw = [exp_tok[kc % 2], sc_free[kc % 2], prod_last, v_last]
                        mt = None
                        for dcc in range(DC):
                            mt = P.op("tensor", lambda e, b=bank, d=dcc, kk=kc, q2=qc:
                                      e.matmul(b[:, :],
                                               kT[:, d * 2048 + kk * 128:d * 2048 + (kk + 1) * 128],
                                               qT[:, d * 2048 + q2 * 512:d * 2048 + (q2 + 1) * 512],
                                               start=(d == 0), stop=(d == DC - 1)),
                                      waits=(bw if dcc == 0 else ()), inc="mm")
                        if pend_cs is not None:
                            kcp, tep = pend_cs
                            csb = 2 + qc % 2
                            w = [tep] + ([sc_free[csb]] if kcp == kts[0] else [])
                            P.op("tensor", lambda e, kk=kcp, bs=base, b2=csb:
                                 e.matmul(sc_b[b2][0:1, 0:512], ones_bf[:, :],
                                          aeT[:, bs + kk * 512:bs + (kk + 1) * 512],
                                          start=(kk == kts[0]), stop=False),
                                 waits=w, inc="mm")
                        wmask = [mt]
                        if 4 * qc <= kc < 4 * qc + 4:
                            r = kc - 4 * qc
                            tm = P.op("vector", lambda e, b=bank, rr=r:
                                      e.tensor_tensor(out=b[:, :], in0=b[:, :],
                                                      in1=mask_sb[:, 384 - rr * 128:896 - rr * 128],
                                                      op=OP.add),
                                      waits=[mt, mask_tok[0]], inc="ve")
                            wmask = [tm]
                        texp = P.op("scalar", lambda e, b=bank, kk=kc, bs=base:
                                    e.activation(out=aeT[:, bs + kk * 512:bs + (kk + 1) * 512],
                                                 in_=b[:, :], func=AF.Exp,
                                                 bias=cm20[:, :], scale=1.0),
                                    waits=wmask + [avm_hist[0]], inc="sc")
                        exp_tok[kc % 2] = texp
                        sc_free[kc % 2] = texp
                        pend_cs = (kc, texp)
                    kcp, tep = pend_cs
                    csb = 2 + qc % 2
                    mt = P.op("tensor", lambda e, kk=kcp, bs=base, b2=csb:
                              e.matmul(sc_b[b2][0:1, 0:512], ones_bf[:, :],
                                       aeT[:, bs + kk * 512:bs + (kk + 1) * 512],
                                       start=(kk == kts[0]), stop=True),
                              waits=[tep] + ([sc_free[csb]] if kcp == kts[0] else []), inc="mm")
                    return dict(qc=qc, kts=kts, base=base, texp=texp, cs=mt)

                def attn_av_chunk(st_):
                    nonlocal aoT_last
                    qc, kts, base, texp, cs = (st_["qc"], st_["kts"], st_["base"],
                                               st_["texp"], st_["cs"])
                    r0 = 8 + (qc % 2) * 4
                    csb = 2 + qc % 2
                    tse = P.op("vector", lambda e, b2=csb: e.tensor_copy(out=sums_sb[:, :],
                                                                         in_=sc_b[b2][0:1, 0:512]),
                               waits=[cs, row_free[0]], inc="ve")
                    sc_free[csb] = tse
                    mtt = None
                    for t in range(4):
                        w = [tse, ep_tok[0]] if t == 0 else ()
                        mtt = P.op("tensor", lambda e, t2=t: e.transpose(
                            av_b[0][:, 508 + t2:508 + t2 + 1],
                            sums_sb[0:1, t2 * 128:(t2 + 1) * 128], identf[:, :]),
                            waits=w, inc="mm")
                    row_free[0] = mtt
                    trc = P.op("vector", lambda e, r2=r0: e.reciprocal(
                        redu[:, r2:r2 + 4], av_b[0][:, 508:512]),
                        waits=[mtt], inc="ve")
                    for t in range(4):
                        i = 4 * qc + t
                        avm = [None, None]
                        for n_, kc in enumerate(kts):
                            for dh in range(2):
                                bw = ()
                                if n_ == 0:
                                    bw = [ep_tok[dh], texp, v_last, trc]
                                avm[dh] = P.op("tensor", lambda e, b=av_b[dh], kk=kc, h=dh, bs=base,
                                               t2=t, first=(n_ == 0), last=(kc == kts[-1]):
                                               e.matmul(b[:, :],
                                                        aeT[:, bs + kk * 512 + t2 * 128:bs + kk * 512 + (t2 + 1) * 128],
                                                        vtm[:, kk * D + h * 512:kk * D + (h + 1) * 512],
                                                        start=first, stop=last),
                                               waits=bw, inc="mm")
                        avm_hist[0] = avm[1]
                        if variant == "a":
                            sl = i % 2
                            xrv = xr[:, sl * D:(sl + 1) * D]
                            tkr = ch_xr.dma(xrv, x_src[i * 128:(i + 1) * 128, :],
                                            waits=[xr_free[sl], store_tok[i]])
                            for dh in range(2):
                                ep_tok[dh] = P.op("vector", lambda e, b=av_b[dh], h=dh, s=sl, r2=r0 + t:
                                                  e.scalar_tensor_tensor(
                                                      out=xnew[:, s * D + h * 512:s * D + (h + 1) * 512],
                                                      in0=b[:, :], scalar=redu[:, r2:r2 + 1],
                                                      in1=xr[:, s * D + h * 512:s * D + (h + 1) * 512],
                                                      op0=OP.mult, op1=OP.add),
                                                  waits=[avm[dh], trc, tkr, xnew_free[sl]], inc="ve")
                            xr_free[sl] = ep_tok[1]
                            st = ch_st.dma(x_dst[i * 128:(i + 1) * 128, :],
                                           xnew[:, sl * D:(sl + 1) * D], waits=[ep_tok[1]])
                            xnew_free[sl] = st
                            new_store[i] = st
                            av_free[0], av_free[1] = ep_tok[0], ep_tok[1]
                        else:
                            tao = [None, None]
                            for dh in range(2):
                                tao[dh] = P.op("vector", lambda e, b=av_b[dh], h=dh, r2=r0 + t:
                                               e.tensor_scalar(
                                                   out=ao_bf[:, h * 512:(h + 1) * 512],
                                                   in0=b[:, :], scalar1=redu[:, r2:r2 + 1],
                                                   scalar2=None, op0=OP.mult),
                                               waits=[avm[dh], trc, ao_free[0]], inc="ve")
                                ep_tok[dh] = tao[dh]
                            bank = tp_b[i % 2]
                            mtr = None
                            for dcc in range(DC):
                                w = [tao[1], at_evac[i % 2]] if dcc == 0 else ()
                                mtr = P.op("tensor", lambda e, b=bank, d=dcc:
                                           e.transpose(b[:, d * 128:(d + 1) * 128],
                                                       ao_bf[:, d * 128:(d + 1) * 128],
                                                       ident[:, :]),
                                           waits=w, inc="mm")
                            aov = aoT[:, :].rearrange("p (c t) -> p c t", c=DC)[:, :, i * 128:(i + 1) * 128]
                            tpv = tp_b[i % 2][:, :].rearrange("p (c t) -> p c t", c=DC)
                            tev = P.op("vector", lambda e, o=aov, s2=tpv: e.tensor_copy(out=o, in_=s2),
                                       waits=[mtr], inc="ve")
                            at_evac[i % 2] = tev
                            ao_free[0] = ("mm", mtr[1])
                            aoT_tok[i] = tev
                            aoT_last = tev
                            av_free[0], av_free[1] = ep_tok[0], ep_tok[1]

                def proj_tiles(q0):
                    for i in range(4 * q0, 4 * q0 + 4):
                        sl = i % 2
                        xrv = xr[:, sl * D:(sl + 1) * D]
                        tkr = ch_xr.dma(xrv, x_src[i * 128:(i + 1) * 128, :],
                                        waits=[xr_free[sl], store_tok[i]])
                        tgx = P.op("gpsimd", lambda e, s=sl: e.tensor_tensor(
                            out=xb[:, :], in0=xr[:, s * D:(s + 1) * D],
                            in1=bpb[:, :], op=OP.add),
                            waits=[tkr, bias_toks["bp"], xb_free[0]], inc="gp")
                        for dh in range(2):
                            bank = av_b[dh]
                            bw = [av_free[dh], aoT_tok[i], wp_t[dh]]
                            mt = None
                            for ec in range(DC):
                                mt = P.op("tensor", lambda e, b=bank, c2=ec, ii=i, h=dh:
                                          e.matmul(b[:, :],
                                                   aoT[:, c2 * 2048 + ii * 128:c2 * 2048 + (ii + 1) * 128],
                                                   wbuf[0][:, c2 * D + h * 512:c2 * D + (h + 1) * 512],
                                                   start=(c2 == 0), stop=(c2 == DC - 1)),
                                          waits=(bw if ec == 0 else ()), inc="mm")
                            t = P.op("vector", lambda e, b=bank, h=dh, s=sl:
                                     e.tensor_tensor(
                                         out=xnew[:, s * D + h * 512:s * D + (h + 1) * 512],
                                         in0=b[:, :],
                                         in1=xb[:, h * 512:(h + 1) * 512],
                                         op=OP.add),
                                     waits=[mt, tgx, xnew_free[sl]], inc="ve")
                            av_free[dh] = t
                            ep_tok[dh] = t
                        xr_free[sl] = av_free[1]
                        xb_free[0] = av_free[1]
                        st = ch_st.dma(x_dst[i * 128:(i + 1) * 128, :],
                                       xnew[:, sl * D:(sl + 1) * D], waits=[av_free[1]])
                        xnew_free[sl] = st
                        new_store[i] = st
                        if i == 3:
                            maybe_prelude(new_store)

                if variant == "a":
                    pend = None
                    for qc in range(NC):
                        cur = attn_scores_chunk(qc)
                        if pend is not None:
                            attn_av_chunk(pend)
                            if pend["qc"] == 1:
                                maybe_prelude(new_store)
                        pend = cur
                    attn_av_chunk(pend)
                else:
                    for qc in range(NC):
                        attn_av_chunk(attn_scores_chunk(qc))
                        if qc >= 1:
                            proj_tiles(qc - 1)
                    proj_tiles(NC - 1)


                tp_free[0], tp_free[1] = at_evac[0], at_evac[1]

        def emit_ffn(sub, x_src, x_dst, new_store):
            nsub = f"s{sub}"
            pop_conv(len(conv_steps))
            with ExitStack() as ph:
                Ep = ph.enter_context
                xlnT = Ep(nc.sbuf_tensor(f"xlnT_{nsub}", [128, DC * 2048], BF16))
                xlnT_cur[0] = xlnT
                wo_sb = Ep(nc.sbuf_tensor(f"wo_{nsub}", [128, 32 * D], BF16))
                interT = Ep(nc.sbuf_tensor(f"it_{nsub}", [128, 32 * 512], BF16))
                wit = [Ep(nc.sbuf_tensor(f"wit{j}_{nsub}", [128, 2048], BF16)) for j in range(3)]
                smT = Ep(nc.sbuf_tensor(f"smT_{nsub}", [32, 2048], BF16))
                ga_bf = Ep(nc.sbuf_tensor(f"ga_{nsub}", [128, 8 * 40], BF16))
                bc_bf = Ep(nc.sbuf_tensor(f"bc_{nsub}", [32, FF], BF16))
                gbb = Ep(nc.sbuf_tensor(f"gbb_{nsub}", [128, 8], F32))
                wob = Ep(nc.sbuf_tensor(f"wob_{nsub}", [128, D], F32))
                logb = Ep(nc.sbuf_tensor(f"logb_{nsub}", [128, 2 * 8], F32))
                mke = Ep(nc.sbuf_tensor(f"mke_{nsub}", [128, 2 * 32], F32))
                sm_bf = Ep(nc.sbuf_tensor(f"smb_{nsub}", [128, 4 * 32], BF16))
                xb = Ep(nc.sbuf_tensor(f"xbf_{nsub}", [128, D], F32))

                wo_t = []
                for g in range(4):
                    out_ap = wo_sb[:, g * 8 * D:(g + 1) * 8 * D].rearrange(
                        "p (c f) -> p c f", c=8)
                    in_ap = wo_sc[g * 8:(g + 1) * 8, :, :].rearrange("c p f -> p c f")
                    wo_t.append(ch_cv[g % 4].dma(out_ap, in_ap,
                                                 waits=[conv_state["done"], conv_state["done2"],
                                                        conv_state["done3"], conv_state["done4"]]))
                tga = ch_w.dma(ga_bf[:, :].rearrange("p (c f) -> p c f", c=DC)[:, :, 0:8],
                               gateWT.rearrange("(c p) f -> p c f", p=128))
                tga = ch_w.dma(ga_bf[:, :].rearrange("p (c f) -> p c f", c=DC)[:, :, 8:40],
                               AallT.rearrange("(c p) f -> p c f", p=128), waits=[tga])
                tbc = ch_w.dma(bc_bf[:, :], BcatT)

                ep_tok = [av_free[0], av_free[1]]
                sm_free = [Z, Z, Z, Z]
                pend_tr = [[]]
                wit_freemm = [Z, Z, Z]
                twi_cur = [Z, Z, Z]
                sm_done = [Z] * NC

                xln_te = [Z] * NT
                if sub == 0:
                    for i in range(4):
                        ln_chain(i, x_src)
                for i in range(4):
                    xln_te[i] = ln_trans(i)
                tgb = rank1_bcast(gb_row, 8, gbb)
                twob = rank1_bcast(wob_row, D, wob)
                for c in range(NC):
                    if c + 1 < NC:
                        for i in range(4 * c + 4, 4 * c + 8):
                            ln_chain(i, x_src)
                    xln_last = xln_te[4 * c + 3]
                    for i in range(4 * c, 4 * c + 4):
                        sl = i % 2
                        jb = 2 + (i % 2)
                        bank = sc_b[jb]
                        bw = [sc_free[jb], xln_te[i], tga]
                        mt = None
                        for dcc in range(DC):
                            mt = P.op("tensor", lambda e, b=bank, d=dcc, ii=i, xl=xlnT_cur[0]: e.matmul(
                                b[:, 0:40],
                                xl[:, d * 2048 + ii * 128:d * 2048 + (ii + 1) * 128],
                                ga_bf[:, d * 40:(d + 1) * 40],
                                start=(d == 0), stop=(d == DC - 1)),
                                waits=(bw if dcc == 0 else ()), inc="mm")
                        t1 = P.op("vector", lambda e, b=bank, s=sl: e.tensor_tensor(
                            out=logb[:, s * 8:(s + 1) * 8], in0=b[:, 0:8], in1=gbb[:, :],
                            op=OP.add),
                            waits=[mt, tgb], inc="ve")
                        t2 = P.op("vector", lambda e, s=sl: e.reduce_max(
                            redu[:, 12:13], logb[:, s * 8:(s + 1) * 8], axis=AX.X),
                            waits=[t1], inc="ve")
                        t3 = P.op("vector", lambda e, s=sl: e.tensor_scalar(
                            out=logb[:, s * 8:(s + 1) * 8], in0=logb[:, s * 8:(s + 1) * 8],
                            scalar1=redu[:, 12:13], scalar2=None, op0=OP.is_equal),
                            waits=[t2], inc="ve")
                        def _t5(e, b=bank, s4=(i % 4), m2=sl):
                            mb = bass.AP(logb[:, :].tensor, m2 * 8, [[16, 128], [0, 4], [1, 8]])
                            return e.tensor_tensor(
                                out=sm_bf[:, s4 * 32:(s4 + 1) * 32].rearrange(
                                    "p (r e2) -> p r e2", r=4),
                                in0=b[:, 8:40].rearrange("p (r e2) -> p r e2", r=4),
                                in1=mb, op=OP.mult)
                        t5 = P.op("vector", _t5, waits=[t3, sm_free[i % 4]], inc="ve")
                        sc_free[jb] = t5
                        pend_tr[0].append((i, i % 4, t5))
                        if i % 4 == 3:
                            mt3 = None
                            for (i2, s4, t52) in pend_tr[0]:
                                mt3 = P.op("tensor", lambda e, s=s4, ii=i2: e.transpose(
                                    tp_b[1][0:32, (ii % 4) * 128:(ii % 4 + 1) * 128],
                                    sm_bf[:, s * 32:(s + 1) * 32], ident[:, :]),
                                    waits=([t52, tp_free[1]] if i2 % 4 == 0 else [t52]),
                                    inc="mm")
                                sm_free[s4] = mt3
                            pend_tr[0] = []
                            t6 = P.op("vector", lambda e, cc=c: e.tensor_copy(
                                out=smT[0:32, cc * 512:(cc + 1) * 512], in_=tp_b[1][0:32, 0:512]),
                                waits=[mt3], inc="ve")
                            tp_free[1] = t6
                            sm_done[c] = t6
                    pop_conv(4)
                    it_s = Z
                    it_v = Z

                    def issue_wi(g):
                        twi_cur[g % 3] = ch_wi[g % 3].dma(
                            wit[g % 3][:, :].rearrange("p (c f) -> p c f", c=2),
                            wi_sc[2 * g:2 * g + 2, :, :].rearrange("c p f -> p c f"),
                            waits=[conv_state["done"], conv_state["done2"],
                                   conv_state["done3"], conv_state["done4"],
                                   wit_freemm[g % 3]])

                    for g in range(2):
                        issue_wi(g)
                    for ft in range(32):
                        jb = ft % 4
                        bank = sc_b[jb]
                        if ft % 2 == 0 and ft // 2 + 2 < 16:
                            issue_wi(ft // 2 + 2)
                        g = ft // 2
                        twi = twi_cur[g % 3]
                        bw = [sc_free[jb], twi, sm_done[c], xln_last, tbc]
                        mt = None
                        for dcc in range(DC):
                            mt = P.op("tensor", lambda e, b=bank, d=dcc, cc=c, f=ft, xl=xlnT_cur[0]: e.matmul(
                                b[:, :], wit[(f // 2) % 3][:, (f % 2) * 1024 + d * 128:(f % 2) * 1024 + (d + 1) * 128],
                                xl[:, d * 2048 + cc * 512:d * 2048 + (cc + 1) * 512],
                                start=(d == 0), stop=False),
                                waits=(bw if dcc == 0 else ()), inc="mm")
                        mt = P.op("tensor", lambda e, b=bank, cc=c, f=ft: e.matmul(
                            b[:, :], bc_bf[0:32, f * 128:(f + 1) * 128],
                            smT[0:32, cc * 512:(cc + 1) * 512],
                            start=False, stop=True), inc="mm")
                        if ft % 2 == 1:
                            wit_freemm[(ft // 2) % 3] = mt
                        if ft % 2 == 0:
                            tev = P.op("scalar", lambda e, b=bank, f=ft: e.activation(
                                out=interT[:, f * 512:(f + 1) * 512], in_=b[:, :], func=AF.Relu,
                                bias=wibs[:, f:f + 1], scale=1.0),
                                waits=[mt, const_tok], inc="sc")
                            it_s = tev
                        else:
                            tev = P.op("vector", lambda e, b=bank, f=ft: e.tensor_scalar(
                                out=interT[:, f * 512:(f + 1) * 512], in0=b[:, :],
                                scalar1=wibs[:, f:f + 1], scalar2=0.0,
                                op0=OP.add, op1=OP.max),
                                waits=[mt, const_tok], inc="ve")
                            it_v = tev
                        sc_free[jb] = tev
                    for tt in range(4):
                        i = 4 * c + tt
                        sl = i % 2
                        xrv = xr[:, sl * D:(sl + 1) * D]
                        tkr = ch_xr.dma(xrv, x_src[i * 128:(i + 1) * 128, :],
                                        waits=[xr_free[sl], store_tok[i]])
                        tgx = P.op("gpsimd", lambda e, s=sl: e.tensor_tensor(
                            out=xb[:, :], in0=xr[:, s * D:(s + 1) * D],
                            in1=wob[:, :], op=OP.add),
                            waits=[tkr, twob, xb_free[0]], inc="gp")
                        for dh in range(2):
                            bank = av_b[dh]
                            bw = [av_free[dh], it_s, it_v, wo_t[0]]
                            mt = None
                            for fc in range(32):
                                mt = P.op("tensor", lambda e, b=bank, f=fc, t2=tt, h=dh: e.matmul(
                                    b[:, :],
                                    interT[:, f * 512 + t2 * 128:f * 512 + (t2 + 1) * 128],
                                    wo_sb[:, f * D + h * 512:f * D + (h + 1) * 512],
                                    start=(f == 0), stop=(f == 31)),
                                    waits=(bw if fc == 0 else [wo_t[fc // 8]]), inc="mm")
                            av_free[dh] = P.op("vector", lambda e, b=bank, h=dh, s=sl:
                                               e.tensor_tensor(
                                                   out=xnew[:, s * D + h * 512:s * D + (h + 1) * 512],
                                                   in0=b[:, :],
                                                   in1=xb[:, h * 512:(h + 1) * 512],
                                                   op=OP.add),
                                               waits=[mt, tgx, xnew_free[sl]], inc="ve")
                        xr_free[sl] = av_free[1]
                        xb_free[0] = av_free[1]
                        st = ch_st.dma(x_dst[i * 128:(i + 1) * 128, :],
                                       xnew[:, sl * D:(sl + 1) * D], waits=[av_free[1]])
                        xnew_free[sl] = st
                        new_store[i] = st
                    if c + 1 < NC:
                        for i2 in range(4 * c + 4, 4 * c + 8):
                            xln_te[i2] = ln_trans(i2)
                    if c == 2:
                        maybe_prelude(new_store)

        def phase_fence():
            toks = [(s, P.counts[s]) for s in ("mm", "ve", "sc", "gp")]
            for eng in ("tensor", "vector", "scalar", "gpsimd"):
                sem = {"tensor": "mm", "vector": "ve", "scalar": "sc", "gpsimd": "gp"}[eng]
                P.op(eng, lambda e: e.nop(),
                     waits=[t for t in toks if t[0] != sem])

        x_src = x_in
        for sub in range(n_sub):
            x_dst = y if sub == n_sub - 1 else (xbufA if sub % 2 == 0 else xbufB)
            kind = SEQ[sub]
            new_store = [Z] * NT
            prelude_info[0] = x_dst if sub != n_sub - 1 else None
            if kind == "f":
                emit_ffn(sub, x_src, x_dst, new_store)
            else:
                emit_attn(sub, x_src, x_dst, kind, new_store)
            store_tok[:] = new_store
            x_src = x_dst
            if sub != n_sub - 1:
                if not prelude_done[0]:
                    for i in range(4):
                        ln_chain(i, x_src)
                phase_fence()
            prelude_done[0] = False
        pop_conv(len(conv_steps))

        P.op("sync", lambda e: e.nop(),
             waits=[ch.last for ch in (ch_x, ch_xr, ch_st, ch_w, ch_wi[0], ch_wi[1],
                                       ch_wi[2], ch_cv[0], ch_cv[1], ch_cv[2],
                                       ch_cv[3], ch_ms)])

        with nc.Block() as block:
            P.emit_all(block)

    return nc


def _prep(inputs):
    f32 = np.float32
    c = np.ascontiguousarray
    d = {}
    d["W_a"] = c(np.asarray(inputs["L0_attn_a_W"], f32))
    d["WqT"] = c(np.asarray(inputs["L0_attn_b_Wq"], f32).T)
    d["WkT"] = c(np.asarray(inputs["L0_attn_b_Wk"], f32).T)
    d["WvT"] = c(np.asarray(inputs["L0_attn_b_Wv"], f32).T)
    d["WpT"] = c(np.asarray(inputs["L0_attn_b_Wp"], f32).T)
    d["bq_c"] = c(np.asarray(inputs["L0_attn_b_bq"], f32).reshape(8, 128).T)
    d["bk_c"] = c(np.asarray(inputs["L0_attn_b_bk"], f32).reshape(8, 128).T)
    d["bv_row"] = c(np.asarray(inputs["L0_attn_b_bv"], f32).reshape(1, D))
    d["bp_row"] = c(np.asarray(inputs["L0_attn_b_bp"], f32).reshape(1, D))
    d["gateWT"] = c(np.asarray(inputs["L0_ffn_gate_W"], f32).T)
    d["gb_row"] = c(np.asarray(inputs["L0_ffn_gate_b"], f32).reshape(1, 8))
    d["wiT"] = c(np.asarray(inputs["L0_ffn_wi_W"], f32).T)
    d["wib_c"] = c(np.asarray(inputs["L0_ffn_wi_b"], f32).reshape(32, 128).T)
    d["woT"] = c(np.asarray(inputs["L0_ffn_wo_W"], f32).T)
    d["wob_row"] = c(np.asarray(inputs["L0_ffn_wo_b"], f32).reshape(1, D))
    d["AallT"] = c(np.asarray(inputs["L0_ffn_lora_A"], f32).transpose(1, 0, 2).reshape(32, D).T)
    d["BcatT"] = c(np.asarray(inputs["L0_ffn_lora_B"], f32).transpose(2, 0, 1).reshape(32, FF))
    p = np.arange(128)[:, None]
    u = np.arange(896)[None, :]
    d["maskA_h"] = np.where(u < p + 384, NEG, 0.0).astype(ml_dtypes.bfloat16)
    d["maskB_h"] = np.where(u > p + 384, NEG, 0.0).astype(ml_dtypes.bfloat16)
    d["ident_h"] = np.eye(128, dtype=ml_dtypes.bfloat16)
    return d


def kernel(**inputs):
    shared = _prep(inputs)
    x = np.asarray(inputs["L0_attn_a_x"], np.float32)
    nc = build_program(N_SUB)
    in_maps = [dict(shared, x_in=np.ascontiguousarray(x[c])) for c in range(8)]
    res = run_bass_kernel_spmd(nc, in_maps, core_ids=list(range(8)))
    return np.stack([np.asarray(res.results[c]["y"]) for c in range(8)], axis=0)


# version 27 (final)
# speedup vs baseline: 1.1845x; 1.0135x over previous
import numpy as np
import ml_dtypes
from contextlib import ExitStack

import concourse.bass as bass
import concourse.mybir as mybir
from concourse.bass_utils import run_bass_kernel_spmd
from concourse.alu_op_type import AluOpType as OP

F32 = mybir.dt.float32
BF16 = mybir.dt.bfloat16
AF = mybir.ActivationFunctionType
AX = mybir.AxisListType

S = 2048
D = 1024
FF = 4096
NT = 16
NC = 4
DC = 8
NEG = -1.0e9
EPS = 1e-5
SCALE = 1.0 / 32.0

N_SUB = 4
SEQ = ["a", "f", "b", "f"]

Z = ("mm", 0)


class Prog:
    ENGS = ("tensor", "vector", "scalar", "gpsimd", "sync")

    def __init__(self, nc, stack):
        self.nc = nc
        self.stack = stack
        self.items = {e: [] for e in self.ENGS}
        self.sems = {}
        self.counts = {}
        for name in ("mm", "ve", "sc", "gp"):
            self.sems[name] = stack.enter_context(nc.semaphore(name))
            self.counts[name] = 0

    def chan(self, name, eng):
        return Chan(self, name, eng)

    def op(self, eng, emit, waits=(), inc=None, n=1):
        tok = None
        if inc is not None:
            self.counts[inc] += n
            tok = (inc, self.counts[inc])
        ws = tuple((s, v) for (s, v) in waits if v > 0)
        self.items[eng].append((emit, ws, inc, n))
        return tok


    def emit_all(self, block):
        for eng_name in self.ENGS:
            seq = self.items[eng_name]
            if not seq:
                continue
            sems = self.sems

            def body(eng, seq=seq):
                last = {}
                for emit, ws, inc, n in seq:
                    for s, v in ws:
                        if last.get(s, 0) < v:
                            eng.wait_ge(sems[s], v)
                            last[s] = v
                    instr = emit(eng)
                    if inc is not None:
                        instr.then_inc(sems[inc], n)
            getattr(block, eng_name)(body)


class Chan:

    def __init__(self, P, name, eng):
        self.P = P
        self.eng = eng
        self.name = name
        P.sems[name] = P.stack.enter_context(P.nc.semaphore(name))
        P.counts[name] = 0
        self.last = (name, 0)

    def dma(self, out, in_, waits=()):
        tok = self.P.op(self.eng, lambda e, o=out, i=in_: e.dma_start(out=o, in_=i),
                        waits=list(waits) + [self.last], inc=self.name, n=16)
        self.last = tok
        return tok


def build_program(n_sub=N_SUB):
    nc = bass.Bass("TRN2", target_bir_lowering=False)
    dt = nc.dram_tensor

    x_in = dt("x_in", [S, D], F32, kind="ExternalInput").ap()
    W_a = dt("W_a", [D, 3 * D], F32, kind="ExternalInput").ap()
    WqT = dt("WqT", [D, D], F32, kind="ExternalInput").ap()
    WkT = dt("WkT", [D, D], F32, kind="ExternalInput").ap()
    WvT = dt("WvT", [D, D], F32, kind="ExternalInput").ap()
    WpT = dt("WpT", [D, D], F32, kind="ExternalInput").ap()
    bq_c = dt("bq_c", [128, 8], F32, kind="ExternalInput").ap()
    bk_c = dt("bk_c", [128, 8], F32, kind="ExternalInput").ap()
    bv_row = dt("bv_row", [1, D], F32, kind="ExternalInput").ap()
    bp_row = dt("bp_row", [1, D], F32, kind="ExternalInput").ap()
    gateWT = dt("gateWT", [D, 8], F32, kind="ExternalInput").ap()
    gb_row = dt("gb_row", [1, 8], F32, kind="ExternalInput").ap()
    wiT = dt("wiT", [D, FF], F32, kind="ExternalInput").ap()
    wib_c = dt("wib_c", [128, 32], F32, kind="ExternalInput").ap()
    woT = dt("woT", [FF, D], F32, kind="ExternalInput").ap()
    wob_row = dt("wob_row", [1, D], F32, kind="ExternalInput").ap()
    AallT = dt("AallT", [D, 32], F32, kind="ExternalInput").ap()
    BcatT = dt("BcatT", [32, FF], F32, kind="ExternalInput").ap()
    maskA_h = dt("maskA_h", [128, 896], BF16, kind="ExternalInput").ap()
    maskB_h = dt("maskB_h", [128, 896], BF16, kind="ExternalInput").ap()
    ident_h = dt("ident_h", [128, 128], BF16, kind="ExternalInput").ap()

    y = dt("y", [S, D], F32, kind="ExternalOutput").ap()
    xbufA = dt("xbufA", [S, D], F32).ap()
    xbufB = dt("xbufB", [S, D], F32).ap()
    wi_sc = dt("wi_sc", [32, 128, 1024], BF16).ap()
    wo_sc = dt("wo_sc", [32, 128, D], BF16).ap()

    with ExitStack() as stk:
        E = stk.enter_context
        P = Prog(nc, stk)
        ch_x = P.chan("cx", "sync")
        ch_xr = P.chan("cr", "sync")
        ch_st = P.chan("cs", "sync")
        ch_w = P.chan("cw", "gpsimd")
        ch_wi = [P.chan("cw0", "sync"), P.chan("cw1", "sync"),
                 P.chan("cw2", "sync")]
        ch_cvg = P.chan("cvg", "gpsimd")
        ch_cvc = P.chan("cvc", "scalar")
        ch_ms = P.chan("cms", "sync")

        sc_b = [E(nc.psum_tensor(f"ps_sc{j}", [128, 512], F32)) for j in range(4)]
        av_b = [E(nc.psum_tensor(f"ps_av{j}", [128, 512], F32)) for j in range(2)]
        tp_b = [E(nc.psum_tensor(f"ps_tp{j}", [128, 1024], BF16)) for j in range(2)]

        xt = E(nc.sbuf_tensor("xt", [128, 2 * D], F32))
        xr = E(nc.sbuf_tensor("xr", [128, 2 * D], F32))
        xnew = E(nc.sbuf_tensor("xnew", [128, 2 * D], F32))
        xlnb = E(nc.sbuf_tensor("xlnb", [128, 4 * D], BF16))
        stat = E(nc.sbuf_tensor("stat", [128, 32], F32))
        smal = E(nc.sbuf_tensor("smal", [128, 32], F32))
        redu = E(nc.sbuf_tensor("redu", [128, 16], F32))
        masks = E(nc.sbuf_tensor("masks", [128, 896], BF16))
        ident = E(nc.sbuf_tensor("ident", [128, 128], BF16))
        ones1 = E(nc.sbuf_tensor("ones1", [1, 128], F32))
        row_s = E(nc.sbuf_tensor("row_s", [1, 512], F32))
        bqs = E(nc.sbuf_tensor("bqs", [128, 8], F32))
        bks = E(nc.sbuf_tensor("bks", [128, 8], F32))
        wibs = E(nc.sbuf_tensor("wibs", [128, 32], F32))

        epsc = E(nc.sbuf_tensor("epsc", [128, 1], F32))
        zerc = E(nc.sbuf_tensor("zerc", [128, 1], F32))
        cm20 = E(nc.sbuf_tensor("cm20", [128, 1], F32))
        ones_bf = E(nc.sbuf_tensor("ones_bf", [128, 1], BF16))
        identf = E(nc.sbuf_tensor("identf", [1, 1], F32))
        for dst, src in [(ident, ident_h), (bqs, bq_c),
                         (bks, bk_c), (wibs, wib_c)]:
            const_tok = ch_ms.dma(dst[:, :], src)
        mask_tok = [Z]
        P.op("vector", lambda e: e.memset(ones1[:, :], 1.0), inc="ve")
        P.op("vector", lambda e: e.memset(epsc[:, :], EPS), inc="ve")
        P.op("vector", lambda e: e.memset(zerc[:, :], 0.0), inc="ve")
        P.op("vector", lambda e: e.memset(cm20[:, :], -20.0), inc="ve")
        P.op("vector", lambda e: e.memset(identf[:, :], 1.0), inc="ve")
        P.op("vector", lambda e: e.memset(ones_bf[:, :], 1.0), inc="ve")

        xt_free = [Z, Z]
        xr_free = [Z, Z]
        xnew_free = [Z, Z]
        xlnb_free = [Z, Z, Z, Z]
        xb_free = [Z, Z]
        ao_free = [Z, Z]
        sc_free = [Z, Z, Z, Z]
        av_free = [Z, Z]
        tp_free = [Z, Z]
        store_tok = [Z] * NT
        xlnT_cur = [None]
        prelude_done = [False]
        prelude_info = [None]

        def maybe_prelude(cur_store):
            if prelude_info[0] is None or prelude_done[0]:
                return
            nxt = prelude_info[0]
            for i in range(4):
                ln_chain(i, nxt, stok=cur_store[i])
            prelude_done[0] = True

        conv_state = {"done": None}
        ch_cv = [P.chan(f"cv{j}", "gpsimd") for j in range(4)]

        def _conv_steps():
            steps = []
            for ft in range(32):
                def go(ft=ft):
                    ch_cv[ft % 4].dma(
                        wi_sc[ft, :, :].rearrange("p (c f) -> p c f", c=DC),
                        wiT[:, ft * 128:(ft + 1) * 128].rearrange("(c p) f -> p c f", p=128))
                steps.append(go)
            for fc in range(32):
                def go(fc=fc):
                    ch_cv[fc % 4].dma(wo_sc[fc, :, :], woT[fc * 128:(fc + 1) * 128, :])
                steps.append(go)

            def fin():
                conv_state["done"] = ch_cv[0].last
                conv_state["done2"] = ch_cv[1].last
                conv_state["done3"] = ch_cv[2].last
                conv_state["done4"] = ch_cv[3].last
            steps.append(fin)
            return steps

        conv_steps = _conv_steps()

        def pop_conv(k):
            for _ in range(min(k, len(conv_steps))):
                conv_steps.pop(0)()

        ln_t5 = [Z] * NT

        def ln_chain(i, x_src, stok=None):
            sl = i % 2
            xtv = xt[:, sl * D:(sl + 1) * D]
            tok_x = ch_x.dma(xtv, x_src[i * 128:(i + 1) * 128, :],
                             waits=[xt_free[sl], store_tok[i] if stok is None else stok])
            stv = stat[:, sl * 16:sl * 16 + 12]
            mv = smal[:, sl * 16:sl * 16 + 2]
            stdv = smal[:, sl * 16 + 2:sl * 16 + 3]
            rstd = smal[:, sl * 16 + 3:sl * 16 + 4]
            P.op("vector", lambda e, a=stv, b=xtv: e.bn_stats(a[:, 0:6], b[:, 0:512]),
                 waits=[tok_x], inc="ve")
            t1 = P.op("vector", lambda e, a=stv, b=xtv: e.bn_stats(a[:, 6:12], b[:, 512:1024]),
                      inc="ve")
            t2 = P.op("vector", lambda e, a=mv, b=stv: e.bn_aggr(a, b),
                      waits=[t1], inc="ve")
            t3 = P.op("scalar", lambda e, a=stdv, b=mv: e.activation(
                out=a, in_=b[:, 1:2], func=AF.Sqrt, bias=epsc[:, :], scale=1.0),
                waits=[t2], inc="sc")
            t4 = P.op("vector", lambda e, a=rstd, b=stdv: e.reciprocal(a, b),
                      waits=[t3], inc="ve")
            xlv = xlnb[:, (i % 4) * D:(i % 4 + 1) * D]
            t5 = P.op("vector", lambda e, o=xlv, a=xtv, m=mv, r=rstd: e.tensor_scalar(
                out=o, in0=a, scalar1=m[:, 0:1], scalar2=r,
                op0=OP.subtract, op1=OP.mult),
                waits=[t4, xlnb_free[i % 4]], inc="ve")
            xt_free[sl] = t5
            ln_t5[i] = t5
            return t5

        def ln_trans(i):
            g = i % 2
            xlv = xlnb[:, (i % 4) * D:(i % 4 + 1) * D]
            mt = None
            for dc in range(DC):
                mt = P.op("tensor", lambda e, d=dc, o=tp_b[g], s=xlv:
                          e.transpose(o[:, d * 128:(d + 1) * 128],
                                      s[:, d * 128:(d + 1) * 128], ident[:, :]),
                          waits=([ln_t5[i], tp_free[g], const_tok] if dc == 0 else ()), inc="mm")
            xlnb_free[i % 4] = mt
            xv = xlnT_cur[0][:, :].rearrange("p (c t) -> p c t", c=DC)[:, :, i * 128:(i + 1) * 128]
            tv = tp_b[g][:, :].rearrange("p (c t) -> p c t", c=DC)
            te = P.op("scalar", lambda e, o=xv, s=tv: e.copy(out=o, in_=s),
                      waits=[mt], inc="sc")
            tp_free[g] = te
            return te

        row_free = [Z]
        sums_free = [Z]

        def rank1_bcast(row_ap, n, out_sb):
            tok = None
            for h in range((n + 511) // 512):
                w = min(512, n - h * 512)
                b = av_b[h % 2]
                tok_r = ch_ms.dma(row_s[:, 0:w], row_ap[0:1, h * 512:h * 512 + w],
                                  waits=[row_free[0]])
                tm = P.op("tensor", lambda e, bb=b, ww=w: e.matmul(
                    bb[:, 0:ww], ones1[:, :], row_s[:, 0:ww],
                    start=True, stop=True),
                    waits=[tok_r, av_free[h % 2], ("ve", 1)], inc="mm")
                row_free[0] = tm
                tok = P.op("vector", lambda e, bb=b, hh=h, ww=w, o=out_sb: e.tensor_copy(
                    out=o[:, hh * 512:hh * 512 + ww], in_=bb[:, 0:ww]),
                    waits=[tm], inc="ve")
                av_free[h % 2] = tok
            return tok

        def emit_attn(sub, x_src, x_dst, variant, new_store):
            nsub = f"s{sub}"
            with ExitStack() as ph:
                Ep = ph.enter_context
                qT = Ep(nc.sbuf_tensor(f"qT_{nsub}", [128, 16384], BF16))
                kT = Ep(nc.sbuf_tensor(f"kT_{nsub}", [128, 16384], BF16))
                vtm = Ep(nc.sbuf_tensor(f"v_{nsub}", [128, 16384], BF16))
                wbuf = [Ep(nc.sbuf_tensor(f"wb0_{nsub}", [128, 12288 if variant == "a" else 8192], BF16))]
                if variant == "b":
                    bvb = Ep(nc.sbuf_tensor(f"bvb_{nsub}", [128, D], BF16))
                    bpb = Ep(nc.sbuf_tensor(f"bpb_{nsub}", [128, D], BF16))
                    ao_bf = Ep(nc.sbuf_tensor(f"ao_{nsub}", [128, D], BF16))
                    xb = Ep(nc.sbuf_tensor(f"xb_{nsub}", [128, D], F32))
                    bias_toks = {}

                def load_w(src_ap, col0, waits=()):
                    toks = []
                    for hh in range(2):
                        out_ap = wbuf[0][:, :].rearrange("p (c f) -> p c f", c=DC)[:, :, hh * 512:(hh + 1) * 512]
                        in_ap = src_ap[:, col0 + hh * 512:col0 + (hh + 1) * 512].rearrange(
                            "(c p) f -> p c f", p=128)
                        toks.append(ch_w.dma(out_ap, in_ap, waits=waits))
                    return toks

                def load_half(src_ap, col0, hh, slot, waits=()):
                    out_ap = wbuf[0][:, slot * 4096:(slot + 1) * 4096].rearrange(
                        "p (c f) -> p c f", c=DC)
                    in_ap = src_ap[:, col0 + hh * 512:col0 + (hh + 1) * 512].rearrange(
                        "(c p) f -> p c f", p=128)
                    return ch_w.dma(out_ap, in_ap, waits=waits)

                xln_tok = [Z] * NT
                prod_last = Z
                bqs_s = [None]
                if variant == "b":
                    bsc = Ep(nc.sbuf_tensor(f"bqs_s_{nsub}", [128, 8], F32))
                    P.op("vector", lambda e: e.tensor_scalar(
                        out=bsc[:, :], in0=bqs[:, :], scalar1=SCALE, scalar2=None,
                        op0=OP.mult), waits=[const_tok], inc="ve")
                    bqs_s[0] = bsc

                def produce_fm(dst, wtok, bias, c, scale=1.0, slots=None):
                    nonlocal prod_last
                    for et in range(8):
                        jb = et % 4
                        bank = sc_b[jb]
                        bw = [sc_free[jb], xln_tok[4 * c + 3], wtok[et // 4]]
                        if slots is None:
                            wcol = lambda d, ee: d * D + ee * 128
                        else:
                            wcol = lambda d, ee: slots[ee // 4] * 4096 + d * 512 + (ee % 4) * 128
                        mt = None
                        for dcc in range(DC):
                            mt = P.op("tensor", lambda e, b=bank, d=dcc, ee=et, cc=c, xl=xlnT_cur[0],
                                      wc=wcol(dcc, et):
                                      e.matmul(b[:, :],
                                               wbuf[0][:, wc:wc + 128],
                                               xl[:, d * 2048 + cc * 512:d * 2048 + (cc + 1) * 512],
                                               start=(d == 0), stop=(d == DC - 1)),
                                      waits=(bw if dcc == 0 else ()), inc="mm")
                        if bias is None and scale == 1.0:
                            tev = P.op("scalar", lambda e, b=bank, o=dst, ee=et, cc=c:
                                       e.copy(out=o[:, ee * 2048 + cc * 512:ee * 2048 + (cc + 1) * 512],
                                              in_=b[:, :]),
                                       waits=[mt], inc="sc")
                        else:
                            bi = zerc if bias is None else bias
                            tev = P.op("scalar", lambda e, b=bank, o=dst, ee=et, cc=c, bi=bi, sc2=scale:
                                       e.activation(out=o[:, ee * 2048 + cc * 512:ee * 2048 + (cc + 1) * 512],
                                                    in_=b[:, :], func=AF.Identity,
                                                    bias=bi[:, ee:ee + 1] if bi is not zerc else bi[:, 0:1],
                                                    scale=sc2),
                                       waits=[mt, const_tok], inc="sc")
                        sc_free[jb] = tev
                        prod_last = tev

                v_last = Z
                with nc.sbuf_tensor(f"xlnT_{nsub}", [128, DC * 2048], BF16) as xlnT:
                    xlnT_cur[0] = xlnT
                    if variant == "a":
                        wq_t = [load_half(W_a, 0, 0, 0), load_half(W_a, 0, 1, 1)]
                        tk0 = load_half(W_a, D, 0, 2)
                    else:
                        wq_t = load_w(WqT, 0)
                    if sub == 0:
                        for i in range(4):
                            ln_chain(i, x_src)
                    for i in range(4):
                        xln_tok[i] = ln_trans(i)
                    for c in range(NC):
                        if c + 1 < NC:
                            for i in range(4 * c + 4, 4 * c + 8):
                                ln_chain(i, x_src)
                        pop_conv(4)
                        produce_fm(qT, wq_t, None if variant == "a" else bqs_s[0], c,
                                   scale=SCALE,
                                   slots=(0, 1) if variant == "a" else None)
                        if c == 0 and variant == "b":
                            bias_toks["bv"] = rank1_bcast(bv_row, D, bvb)
                            bias_toks["bp"] = rank1_bcast(bp_row, D, bpb)
                        if c + 1 < NC:
                            for i in range(4 * c + 4, 4 * c + 8):
                                xln_tok[i] = ln_trans(i)
                    if variant == "a":
                        wk_t = [tk0, load_half(W_a, D, 1, 0, waits=[prod_last])]
                        tv0 = load_half(W_a, 2 * D, 0, 1, waits=[prod_last])
                    else:
                        wk_t = load_w(WkT, 0, waits=[prod_last])
                    for c in range(NC):
                        produce_fm(kT, wk_t, None if variant == "a" else bks, c,
                                   slots=(2, 0) if variant == "a" else None)
                    if variant == "a":
                        wv_t = [tv0, load_half(W_a, 2 * D, 1, 2, waits=[prod_last])]
                        v_slots = (1, 2)
                    else:
                        wv_t = load_w(WvT, 0, waits=[prod_last])
                        v_slots = None
                    for i in range(NT):
                        pop_conv(2)
                        for dh in range(2):
                            bank = av_b[dh]
                            bw = [av_free[dh], xln_tok[i], wv_t[dh]]
                            mt = None
                            for dcc in range(DC):
                                if v_slots is None:
                                    wc = dcc * D + dh * 512
                                else:
                                    wc = v_slots[dh] * 4096 + dcc * 512
                                mt = P.op("tensor", lambda e, b=bank, d=dcc, ii=i, h=dh, xl=xlnT_cur[0],
                                          wc2=wc:
                                          e.matmul(b[:, :],
                                                   xl[:, d * 2048 + ii * 128:d * 2048 + (ii + 1) * 128],
                                                   wbuf[0][:, wc2:wc2 + 512],
                                                   start=(d == 0), stop=(d == DC - 1)),
                                          waits=(bw if dcc == 0 else ()), inc="mm")
                            if variant == "a":
                                tev = P.op("scalar", lambda e, b=bank, ii=i, h=dh:
                                           e.copy(out=vtm[:, ii * D + h * 512:ii * D + (h + 1) * 512],
                                                  in_=b[:, :]),
                                           waits=[mt], inc="sc")
                            else:
                                tev = P.op("vector", lambda e, b=bank, ii=i, h=dh:
                                           e.tensor_tensor(out=vtm[:, ii * D + h * 512:ii * D + (h + 1) * 512],
                                                           in0=b[:, :], in1=bvb[:, h * 512:(h + 1) * 512],
                                                           op=OP.add),
                                           waits=[mt, bias_toks["bv"]], inc="ve")
                            av_free[dh] = tev
                            v_last = tev

                aoT = Ep(nc.sbuf_tensor(f"aoT_{nsub}", [128, DC * 2048], BF16)) if variant == "b" else None

                if variant == "b":
                    wp_t = load_w(WpT, 0, waits=[("mm", P.counts["mm"])])

                mask_tok[0] = ch_ms.dma(masks[:, :], maskB_h if variant == "a" else maskA_h)
                mask_sb = masks
                nslot = 2 if variant == "a" else 1
                aeT = Ep(nc.sbuf_tensor(f"aeT_{nsub}", [128, nslot * 8192], BF16))
                sums_sb = row_s
                exp_tok = [Z, Z, Z]
                ep_tok = [av_free[0], av_free[1]]
                at_evac = [tp_free[0], tp_free[1]]
                aoT_last = Z
                aoT_tok = [Z] * NT
                avm_hist = [Z]

                def attn_scores_chunk(qc):
                    pop_conv(4)
                    kts = list(range(4 * qc, NT)) if variant == "a" else list(range(0, 4 * qc + 4))
                    base = (qc % 2) * 8192 if variant == "a" else 0
                    texp = Z
                    pend_cs = []
                    for kc in kts:
                        bank = sc_b[kc % 2]
                        bw = [exp_tok[kc % 2], sc_free[kc % 2], prod_last, v_last]
                        mt = None
                        for dcc in range(DC):
                            mt = P.op("tensor", lambda e, b=bank, d=dcc, kk=kc, q2=qc:
                                      e.matmul(b[:, :],
                                               kT[:, d * 2048 + kk * 128:d * 2048 + (kk + 1) * 128],
                                               qT[:, d * 2048 + q2 * 512:d * 2048 + (q2 + 1) * 512],
                                               start=(d == 0), stop=(d == DC - 1)),
                                      waits=(bw if dcc == 0 else ()), inc="mm")
                        if len(pend_cs) >= 2:
                            kcp, tep = pend_cs.pop(0)
                            csb = 2 + qc % 2
                            w = [tep] + ([sc_free[csb]] if kcp == kts[0] else [])
                            P.op("tensor", lambda e, kk=kcp, bs=base, b2=csb:
                                 e.matmul(sc_b[b2][0:1, 0:512], ones_bf[:, :],
                                          aeT[:, bs + kk * 512:bs + (kk + 1) * 512],
                                          start=(kk == kts[0]), stop=False),
                                 waits=w, inc="mm")
                        wmask = [mt]
                        if 4 * qc <= kc < 4 * qc + 4:
                            r = kc - 4 * qc
                            tm = P.op("vector", lambda e, b=bank, rr=r:
                                      e.tensor_tensor(out=b[:, :], in0=b[:, :],
                                                      in1=mask_sb[:, 384 - rr * 128:896 - rr * 128],
                                                      op=OP.add),
                                      waits=[mt, mask_tok[0]], inc="ve")
                            wmask = [tm]
                        texp = P.op("scalar", lambda e, b=bank, kk=kc, bs=base:
                                    e.activation(out=aeT[:, bs + kk * 512:bs + (kk + 1) * 512],
                                                 in_=b[:, :], func=AF.Exp,
                                                 bias=cm20[:, :], scale=1.0),
                                    waits=wmask + [avm_hist[0]], inc="sc")
                        exp_tok[kc % 2] = texp
                        sc_free[kc % 2] = texp
                        pend_cs.append((kc, texp))
                    csb = 2 + qc % 2
                    mt = None
                    for n_, (kcp, tep) in enumerate(pend_cs):
                        mt = P.op("tensor", lambda e, kk=kcp, bs=base, b2=csb,
                                  last=(n_ == len(pend_cs) - 1):
                                  e.matmul(sc_b[b2][0:1, 0:512], ones_bf[:, :],
                                           aeT[:, bs + kk * 512:bs + (kk + 1) * 512],
                                           start=(kk == kts[0]), stop=last),
                                  waits=[tep] + ([sc_free[csb]] if kcp == kts[0] else []), inc="mm")
                    return dict(qc=qc, kts=kts, base=base, texp=texp, cs=mt)

                def attn_av_chunk(st_):
                    nonlocal aoT_last
                    qc, kts, base, texp, cs = (st_["qc"], st_["kts"], st_["base"],
                                               st_["texp"], st_["cs"])
                    r0 = 8 + (qc % 2) * 4
                    csb = 2 + qc % 2
                    tse = P.op("vector", lambda e, b2=csb: e.tensor_copy(out=sums_sb[:, :],
                                                                         in_=sc_b[b2][0:1, 0:512]),
                               waits=[cs, row_free[0]], inc="ve")
                    sc_free[csb] = tse
                    mtt = None
                    for t in range(4):
                        w = [tse, ep_tok[0]] if t == 0 else ()
                        mtt = P.op("tensor", lambda e, t2=t: e.transpose(
                            av_b[0][:, 508 + t2:508 + t2 + 1],
                            sums_sb[0:1, t2 * 128:(t2 + 1) * 128], identf[:, :]),
                            waits=w, inc="mm")
                    row_free[0] = mtt
                    trc = P.op("vector", lambda e, r2=r0: e.reciprocal(
                        redu[:, r2:r2 + 4], av_b[0][:, 508:512]),
                        waits=[mtt], inc="ve")
                    for t in range(4):
                        i = 4 * qc + t
                        avm = [None, None]
                        for n_, kc in enumerate(kts):
                            for dh in range(2):
                                bw = ()
                                if n_ == 0:
                                    bw = [ep_tok[dh], texp, v_last, trc]
                                avm[dh] = P.op("tensor", lambda e, b=av_b[dh], kk=kc, h=dh, bs=base,
                                               t2=t, first=(n_ == 0), last=(kc == kts[-1]):
                                               e.matmul(b[:, :],
                                                        aeT[:, bs + kk * 512 + t2 * 128:bs + kk * 512 + (t2 + 1) * 128],
                                                        vtm[:, kk * D + h * 512:kk * D + (h + 1) * 512],
                                                        start=first, stop=last),
                                               waits=bw, inc="mm")
                        avm_hist[0] = avm[1]
                        if variant == "a":
                            sl = i % 2
                            xrv = xr[:, sl * D:(sl + 1) * D]
                            tkr = ch_xr.dma(xrv, x_src[i * 128:(i + 1) * 128, :],
                                            waits=[xr_free[sl], store_tok[i]])
                            for dh in range(2):
                                ep_tok[dh] = P.op("vector", lambda e, b=av_b[dh], h=dh, s=sl, r2=r0 + t:
                                                  e.scalar_tensor_tensor(
                                                      out=xnew[:, s * D + h * 512:s * D + (h + 1) * 512],
                                                      in0=b[:, :], scalar=redu[:, r2:r2 + 1],
                                                      in1=xr[:, s * D + h * 512:s * D + (h + 1) * 512],
                                                      op0=OP.mult, op1=OP.add),
                                                  waits=[avm[dh], trc, tkr, xnew_free[sl]], inc="ve")
                            xr_free[sl] = ep_tok[1]
                            st = ch_st.dma(x_dst[i * 128:(i + 1) * 128, :],
                                           xnew[:, sl * D:(sl + 1) * D], waits=[ep_tok[1]])
                            xnew_free[sl] = st
                            new_store[i] = st
                            av_free[0], av_free[1] = ep_tok[0], ep_tok[1]
                        else:
                            tao = [None, None]
                            for dh in range(2):
                                tao[dh] = P.op("vector", lambda e, b=av_b[dh], h=dh, r2=r0 + t:
                                               e.tensor_scalar(
                                                   out=ao_bf[:, h * 512:(h + 1) * 512],
                                                   in0=b[:, :], scalar1=redu[:, r2:r2 + 1],
                                                   scalar2=None, op0=OP.mult),
                                               waits=[avm[dh], trc, ao_free[0]], inc="ve")
                                ep_tok[dh] = tao[dh]
                            bank = tp_b[i % 2]
                            mtr = None
                            for dcc in range(DC):
                                w = [tao[1], at_evac[i % 2]] if dcc == 0 else ()
                                mtr = P.op("tensor", lambda e, b=bank, d=dcc:
                                           e.transpose(b[:, d * 128:(d + 1) * 128],
                                                       ao_bf[:, d * 128:(d + 1) * 128],
                                                       ident[:, :]),
                                           waits=w, inc="mm")
                            aov = aoT[:, :].rearrange("p (c t) -> p c t", c=DC)[:, :, i * 128:(i + 1) * 128]
                            tpv = tp_b[i % 2][:, :].rearrange("p (c t) -> p c t", c=DC)
                            tev = P.op("vector", lambda e, o=aov, s2=tpv: e.tensor_copy(out=o, in_=s2),
                                       waits=[mtr], inc="ve")
                            at_evac[i % 2] = tev
                            ao_free[0] = ("mm", mtr[1])
                            aoT_tok[i] = tev
                            aoT_last = tev
                            av_free[0], av_free[1] = ep_tok[0], ep_tok[1]

                def proj_tiles(q0):
                    for i in range(4 * q0, 4 * q0 + 4):
                        sl = i % 2
                        xrv = xr[:, sl * D:(sl + 1) * D]
                        tkr = ch_xr.dma(xrv, x_src[i * 128:(i + 1) * 128, :],
                                        waits=[xr_free[sl], store_tok[i]])
                        tgx = P.op("gpsimd", lambda e, s=sl: e.tensor_tensor(
                            out=xb[:, :], in0=xr[:, s * D:(s + 1) * D],
                            in1=bpb[:, :], op=OP.add),
                            waits=[tkr, bias_toks["bp"], xb_free[0]], inc="gp")
                        for dh in range(2):
                            bank = av_b[dh]
                            bw = [av_free[dh], aoT_tok[i], wp_t[dh]]
                            mt = None
                            for ec in range(DC):
                                mt = P.op("tensor", lambda e, b=bank, c2=ec, ii=i, h=dh:
                                          e.matmul(b[:, :],
                                                   aoT[:, c2 * 2048 + ii * 128:c2 * 2048 + (ii + 1) * 128],
                                                   wbuf[0][:, c2 * D + h * 512:c2 * D + (h + 1) * 512],
                                                   start=(c2 == 0), stop=(c2 == DC - 1)),
                                          waits=(bw if ec == 0 else ()), inc="mm")
                            t = P.op("vector", lambda e, b=bank, h=dh, s=sl:
                                     e.tensor_tensor(
                                         out=xnew[:, s * D + h * 512:s * D + (h + 1) * 512],
                                         in0=b[:, :],
                                         in1=xb[:, h * 512:(h + 1) * 512],
                                         op=OP.add),
                                     waits=[mt, tgx, xnew_free[sl]], inc="ve")
                            av_free[dh] = t
                            ep_tok[dh] = t
                        xr_free[sl] = av_free[1]
                        xb_free[0] = av_free[1]
                        st = ch_st.dma(x_dst[i * 128:(i + 1) * 128, :],
                                       xnew[:, sl * D:(sl + 1) * D], waits=[av_free[1]])
                        xnew_free[sl] = st
                        new_store[i] = st
                        if i == 3:
                            maybe_prelude(new_store)

                if variant == "a":
                    pend = None
                    for qc in range(NC):
                        cur = attn_scores_chunk(qc)
                        if pend is not None:
                            attn_av_chunk(pend)
                            if pend["qc"] == 1:
                                maybe_prelude(new_store)
                        pend = cur
                    attn_av_chunk(pend)
                else:
                    for qc in range(NC):
                        attn_av_chunk(attn_scores_chunk(qc))
                        if qc >= 1:
                            proj_tiles(qc - 1)
                    proj_tiles(NC - 1)


                tp_free[0], tp_free[1] = at_evac[0], at_evac[1]

        def emit_ffn(sub, x_src, x_dst, new_store):
            nsub = f"s{sub}"
            pop_conv(len(conv_steps))
            with ExitStack() as ph:
                Ep = ph.enter_context
                xlnT = Ep(nc.sbuf_tensor(f"xlnT_{nsub}", [128, DC * 2048], BF16))
                xlnT_cur[0] = xlnT
                wo_sb = Ep(nc.sbuf_tensor(f"wo_{nsub}", [128, 32 * D], BF16))
                interT = Ep(nc.sbuf_tensor(f"it_{nsub}", [128, 32 * 512], BF16))
                wit = [Ep(nc.sbuf_tensor(f"wit{j}_{nsub}", [128, 2048], BF16)) for j in range(3)]
                smT = Ep(nc.sbuf_tensor(f"smT_{nsub}", [32, 2048], BF16))
                ga_bf = Ep(nc.sbuf_tensor(f"ga_{nsub}", [128, 8 * 40], BF16))
                bc_bf = Ep(nc.sbuf_tensor(f"bc_{nsub}", [32, FF], BF16))
                gbb = Ep(nc.sbuf_tensor(f"gbb_{nsub}", [128, 8], F32))
                wob = Ep(nc.sbuf_tensor(f"wob_{nsub}", [128, D], F32))
                logb = Ep(nc.sbuf_tensor(f"logb_{nsub}", [128, 2 * 8], F32))
                mke = Ep(nc.sbuf_tensor(f"mke_{nsub}", [128, 2 * 32], F32))
                sm_bf = Ep(nc.sbuf_tensor(f"smb_{nsub}", [128, 4 * 32], BF16))
                xb = Ep(nc.sbuf_tensor(f"xbf_{nsub}", [128, D], F32))

                wo_t = []
                for g in range(4):
                    out_ap = wo_sb[:, g * 8 * D:(g + 1) * 8 * D].rearrange(
                        "p (c f) -> p c f", c=8)
                    in_ap = wo_sc[g * 8:(g + 1) * 8, :, :].rearrange("c p f -> p c f")
                    wo_t.append(ch_cv[g % 4].dma(out_ap, in_ap,
                                                 waits=[conv_state["done"], conv_state["done2"],
                                                        conv_state["done3"], conv_state["done4"]]))
                tga = ch_w.dma(ga_bf[:, :].rearrange("p (c f) -> p c f", c=DC)[:, :, 0:8],
                               gateWT.rearrange("(c p) f -> p c f", p=128))
                tga = ch_w.dma(ga_bf[:, :].rearrange("p (c f) -> p c f", c=DC)[:, :, 8:40],
                               AallT.rearrange("(c p) f -> p c f", p=128), waits=[tga])
                tbc = ch_w.dma(bc_bf[:, :], BcatT)

                ep_tok = [av_free[0], av_free[1]]
                sm_free = [Z, Z, Z, Z]
                pend_tr = [[]]
                wit_freemm = [Z, Z, Z]
                twi_cur = [Z, Z, Z]
                sm_done = [Z] * NC

                xln_te = [Z] * NT
                if sub == 0:
                    for i in range(4):
                        ln_chain(i, x_src)
                for i in range(4):
                    xln_te[i] = ln_trans(i)
                tgb = rank1_bcast(gb_row, 8, gbb)
                twob = rank1_bcast(wob_row, D, wob)
                for c in range(NC):
                    if c + 1 < NC:
                        for i in range(4 * c + 4, 4 * c + 8):
                            ln_chain(i, x_src)
                    xln_last = xln_te[4 * c + 3]
                    for i in range(4 * c, 4 * c + 4):
                        sl = i % 2
                        jb = 2 + (i % 2)
                        bank = sc_b[jb]
                        bw = [sc_free[jb], xln_te[i], tga]
                        mt = None
                        for dcc in range(DC):
                            mt = P.op("tensor", lambda e, b=bank, d=dcc, ii=i, xl=xlnT_cur[0]: e.matmul(
                                b[:, 0:40],
                                xl[:, d * 2048 + ii * 128:d * 2048 + (ii + 1) * 128],
                                ga_bf[:, d * 40:(d + 1) * 40],
                                start=(d == 0), stop=(d == DC - 1)),
                                waits=(bw if dcc == 0 else ()), inc="mm")
                        t1 = P.op("vector", lambda e, b=bank, s=sl: e.tensor_tensor(
                            out=logb[:, s * 8:(s + 1) * 8], in0=b[:, 0:8], in1=gbb[:, :],
                            op=OP.add),
                            waits=[mt, tgb], inc="ve")
                        t2 = P.op("vector", lambda e, s=sl: e.reduce_max(
                            redu[:, 12:13], logb[:, s * 8:(s + 1) * 8], axis=AX.X),
                            waits=[t1], inc="ve")
                        t3 = P.op("vector", lambda e, s=sl: e.tensor_scalar(
                            out=logb[:, s * 8:(s + 1) * 8], in0=logb[:, s * 8:(s + 1) * 8],
                            scalar1=redu[:, 12:13], scalar2=None, op0=OP.is_equal),
                            waits=[t2], inc="ve")
                        def _t5(e, b=bank, s4=(i % 4), m2=sl):
                            mb = bass.AP(logb[:, :].tensor, m2 * 8, [[16, 128], [0, 4], [1, 8]])
                            return e.tensor_tensor(
                                out=sm_bf[:, s4 * 32:(s4 + 1) * 32].rearrange(
                                    "p (r e2) -> p r e2", r=4),
                                in0=b[:, 8:40].rearrange("p (r e2) -> p r e2", r=4),
                                in1=mb, op=OP.mult)
                        t5 = P.op("vector", _t5, waits=[t3, sm_free[i % 4]], inc="ve")
                        sc_free[jb] = t5
                        pend_tr[0].append((i, i % 4, t5))
                        if i % 4 == 3:
                            mt3 = None
                            for (i2, s4, t52) in pend_tr[0]:
                                mt3 = P.op("tensor", lambda e, s=s4, ii=i2: e.transpose(
                                    tp_b[1][0:32, (ii % 4) * 128:(ii % 4 + 1) * 128],
                                    sm_bf[:, s * 32:(s + 1) * 32], ident[:, :]),
                                    waits=([t52, tp_free[1]] if i2 % 4 == 0 else [t52]),
                                    inc="mm")
                                sm_free[s4] = mt3
                            pend_tr[0] = []
                            t6 = P.op("vector", lambda e, cc=c: e.tensor_copy(
                                out=smT[0:32, cc * 512:(cc + 1) * 512], in_=tp_b[1][0:32, 0:512]),
                                waits=[mt3], inc="ve")
                            tp_free[1] = t6
                            sm_done[c] = t6
                    pop_conv(4)
                    it_s = Z
                    it_v = Z

                    def issue_wi(g):
                        twi_cur[g % 3] = ch_wi[g % 3].dma(
                            wit[g % 3][:, :].rearrange("p (c f) -> p c f", c=2),
                            wi_sc[2 * g:2 * g + 2, :, :].rearrange("c p f -> p c f"),
                            waits=[conv_state["done"], conv_state["done2"],
                                   conv_state["done3"], conv_state["done4"],
                                   wit_freemm[g % 3]])

                    for g in range(2):
                        issue_wi(g)
                    for ft in range(32):
                        jb = ft % 4
                        bank = sc_b[jb]
                        if ft % 2 == 0 and ft // 2 + 2 < 16:
                            issue_wi(ft // 2 + 2)
                        g = ft // 2
                        twi = twi_cur[g % 3]
                        bw = [sc_free[jb], twi, sm_done[c], xln_last, tbc]
                        mt = None
                        for dcc in range(DC):
                            mt = P.op("tensor", lambda e, b=bank, d=dcc, cc=c, f=ft, xl=xlnT_cur[0]: e.matmul(
                                b[:, :], wit[(f // 2) % 3][:, (f % 2) * 1024 + d * 128:(f % 2) * 1024 + (d + 1) * 128],
                                xl[:, d * 2048 + cc * 512:d * 2048 + (cc + 1) * 512],
                                start=(d == 0), stop=False),
                                waits=(bw if dcc == 0 else ()), inc="mm")
                        mt = P.op("tensor", lambda e, b=bank, cc=c, f=ft: e.matmul(
                            b[:, :], bc_bf[0:32, f * 128:(f + 1) * 128],
                            smT[0:32, cc * 512:(cc + 1) * 512],
                            start=False, stop=True), inc="mm")
                        if ft % 2 == 1:
                            wit_freemm[(ft // 2) % 3] = mt
                        if ft % 2 == 0:
                            tev = P.op("scalar", lambda e, b=bank, f=ft: e.activation(
                                out=interT[:, f * 512:(f + 1) * 512], in_=b[:, :], func=AF.Relu,
                                bias=wibs[:, f:f + 1], scale=1.0),
                                waits=[mt, const_tok], inc="sc")
                            it_s = tev
                        else:
                            tev = P.op("vector", lambda e, b=bank, f=ft: e.tensor_scalar(
                                out=interT[:, f * 512:(f + 1) * 512], in0=b[:, :],
                                scalar1=wibs[:, f:f + 1], scalar2=0.0,
                                op0=OP.add, op1=OP.max),
                                waits=[mt, const_tok], inc="ve")
                            it_v = tev
                        sc_free[jb] = tev
                    for tt in range(4):
                        i = 4 * c + tt
                        sl = i % 2
                        xrv = xr[:, sl * D:(sl + 1) * D]
                        tkr = ch_xr.dma(xrv, x_src[i * 128:(i + 1) * 128, :],
                                        waits=[xr_free[sl], store_tok[i]])
                        tgx = P.op("gpsimd", lambda e, s=sl: e.tensor_tensor(
                            out=xb[:, :], in0=xr[:, s * D:(s + 1) * D],
                            in1=wob[:, :], op=OP.add),
                            waits=[tkr, twob, xb_free[0]], inc="gp")
                        for dh in range(2):
                            bank = av_b[dh]
                            bw = [av_free[dh], it_s, it_v, wo_t[0]]
                            mt = None
                            for fc in range(32):
                                mt = P.op("tensor", lambda e, b=bank, f=fc, t2=tt, h=dh: e.matmul(
                                    b[:, :],
                                    interT[:, f * 512 + t2 * 128:f * 512 + (t2 + 1) * 128],
                                    wo_sb[:, f * D + h * 512:f * D + (h + 1) * 512],
                                    start=(f == 0), stop=(f == 31)),
                                    waits=(bw if fc == 0 else [wo_t[fc // 8]]), inc="mm")
                            av_free[dh] = P.op("vector", lambda e, b=bank, h=dh, s=sl:
                                               e.tensor_tensor(
                                                   out=xnew[:, s * D + h * 512:s * D + (h + 1) * 512],
                                                   in0=b[:, :],
                                                   in1=xb[:, h * 512:(h + 1) * 512],
                                                   op=OP.add),
                                               waits=[mt, tgx, xnew_free[sl]], inc="ve")
                        xr_free[sl] = av_free[1]
                        xb_free[0] = av_free[1]
                        st = ch_st.dma(x_dst[i * 128:(i + 1) * 128, :],
                                       xnew[:, sl * D:(sl + 1) * D], waits=[av_free[1]])
                        xnew_free[sl] = st
                        new_store[i] = st
                    if c + 1 < NC:
                        for i2 in range(4 * c + 4, 4 * c + 8):
                            xln_te[i2] = ln_trans(i2)
                    if c == 2:
                        maybe_prelude(new_store)

        def phase_fence():
            toks = [(s, P.counts[s]) for s in ("mm", "ve", "sc", "gp")]
            for eng in ("tensor", "vector", "scalar", "gpsimd"):
                sem = {"tensor": "mm", "vector": "ve", "scalar": "sc", "gpsimd": "gp"}[eng]
                P.op(eng, lambda e: e.nop(),
                     waits=[t for t in toks if t[0] != sem])

        x_src = x_in
        for sub in range(n_sub):
            x_dst = y if sub == n_sub - 1 else (xbufA if sub % 2 == 0 else xbufB)
            kind = SEQ[sub]
            new_store = [Z] * NT
            prelude_info[0] = x_dst if sub != n_sub - 1 else None
            if kind == "f":
                emit_ffn(sub, x_src, x_dst, new_store)
            else:
                emit_attn(sub, x_src, x_dst, kind, new_store)
            store_tok[:] = new_store
            x_src = x_dst
            if sub != n_sub - 1:
                if not prelude_done[0]:
                    for i in range(4):
                        ln_chain(i, x_src)
                phase_fence()
            prelude_done[0] = False
        pop_conv(len(conv_steps))

        P.op("sync", lambda e: e.nop(),
             waits=[ch.last for ch in (ch_x, ch_xr, ch_st, ch_w, ch_wi[0], ch_wi[1],
                                       ch_wi[2], ch_cv[0], ch_cv[1], ch_cv[2],
                                       ch_cv[3], ch_ms)])

        with nc.Block() as block:
            P.emit_all(block)

    return nc


def _prep(inputs):
    f32 = np.float32
    c = np.ascontiguousarray
    d = {}
    d["W_a"] = c(np.asarray(inputs["L0_attn_a_W"], f32))
    d["WqT"] = c(np.asarray(inputs["L0_attn_b_Wq"], f32).T)
    d["WkT"] = c(np.asarray(inputs["L0_attn_b_Wk"], f32).T)
    d["WvT"] = c(np.asarray(inputs["L0_attn_b_Wv"], f32).T)
    d["WpT"] = c(np.asarray(inputs["L0_attn_b_Wp"], f32).T)
    d["bq_c"] = c(np.asarray(inputs["L0_attn_b_bq"], f32).reshape(8, 128).T)
    d["bk_c"] = c(np.asarray(inputs["L0_attn_b_bk"], f32).reshape(8, 128).T)
    d["bv_row"] = c(np.asarray(inputs["L0_attn_b_bv"], f32).reshape(1, D))
    d["bp_row"] = c(np.asarray(inputs["L0_attn_b_bp"], f32).reshape(1, D))
    d["gateWT"] = c(np.asarray(inputs["L0_ffn_gate_W"], f32).T)
    d["gb_row"] = c(np.asarray(inputs["L0_ffn_gate_b"], f32).reshape(1, 8))
    d["wiT"] = c(np.asarray(inputs["L0_ffn_wi_W"], f32).T)
    d["wib_c"] = c(np.asarray(inputs["L0_ffn_wi_b"], f32).reshape(32, 128).T)
    d["woT"] = c(np.asarray(inputs["L0_ffn_wo_W"], f32).T)
    d["wob_row"] = c(np.asarray(inputs["L0_ffn_wo_b"], f32).reshape(1, D))
    d["AallT"] = c(np.asarray(inputs["L0_ffn_lora_A"], f32).transpose(1, 0, 2).reshape(32, D).T)
    d["BcatT"] = c(np.asarray(inputs["L0_ffn_lora_B"], f32).transpose(2, 0, 1).reshape(32, FF))
    p = np.arange(128)[:, None]
    u = np.arange(896)[None, :]
    d["maskA_h"] = np.where(u < p + 384, NEG, 0.0).astype(ml_dtypes.bfloat16)
    d["maskB_h"] = np.where(u > p + 384, NEG, 0.0).astype(ml_dtypes.bfloat16)
    d["ident_h"] = np.eye(128, dtype=ml_dtypes.bfloat16)
    return d


def kernel(**inputs):
    shared = _prep(inputs)
    x = np.asarray(inputs["L0_attn_a_x"], np.float32)
    nc = build_program(N_SUB)
    in_maps = [dict(shared, x_in=np.ascontiguousarray(x[c])) for c in range(8)]
    res = run_bass_kernel_spmd(nc, in_maps, core_ids=list(range(8)))
    return np.stack([np.asarray(res.results[c]["y"]) for c in range(8)], axis=0)
